# version 09
# baseline (speedup 1.0000x reference)
import math
from contextlib import ExitStack

import numpy as np
import ml_dtypes
import concourse.bass as bass
import concourse.mybir as mybir
from concourse.bass_utils import run_bass_kernel_spmd

F32 = mybir.dt.float32
BF16 = mybir.dt.bfloat16
AF = mybir.ActivationFunctionType
ALU = mybir.AluOpType
AX = mybir.AxisListType

D = 1024
DC = D // 128
HID = 2816
E = 8
H = 8
HD = 128
NPAIR = 22
GROUPS = (8, 8, 6)
NMD = 8
EPS_LN = 1e-5
EPS_RMS = 1e-6
NEG_BIG = -1.0e30
NGU = 3
NDN = 2


class Ctr:
    def __init__(self, sem, step=1):
        self.sem = sem
        self.n = 0
        self.step = step


class Sched:
    ENGINES = ("tensor", "scalar", "vector", "gpsimd", "sync")

    def __init__(self):
        self.ops = {e: [] for e in self.ENGINES}
        self.last_wait = {e: {} for e in self.ENGINES}

    def wait(self, eng, ctr, val):
        if val is None or val <= 0:
            return
        lw = self.last_wait[eng]
        k = id(ctr)
        if lw.get(k, 0) >= val:
            return
        lw[k] = val
        self.ops[eng].append(("w", ctr.sem, val))

    def waitt(self, eng, ticket):
        if ticket is not None:
            self.wait(eng, ticket[0], ticket[1])

    def op(self, eng, fn, inc=None):
        self.ops[eng].append(("o", fn, inc))
        if inc is not None:
            inc.n += inc.step
            return inc.n

    def drain(self, eng):
        self.ops[eng].append(("d", None, None))

    def emit(self, nc, name):
        with nc.Block(name) as blk:
            for ename in self.ENGINES:
                ops = self.ops[ename]
                if not ops:
                    continue

                def body(e, ops=ops):
                    for kind, a, b in ops:
                        if kind == "w":
                            e.wait_ge(a, b)
                        elif kind == "d":
                            e.drain()
                        else:
                            inst = a(e)
                            if b is not None:
                                inst.then_inc(b.sem, b.step)
                getattr(blk, ename)(body)


def build_program(T, parts=(1, 2, 3)):
    assert T % 128 == 0 and 128 <= T <= 2048
    QT = T // 128
    TB = min(512, T)
    NB = T // TB
    TH = min(1024, T)
    NH = T // TH
    SW = min(512, TH)
    NS = TH // SW

    nc = bass.Bass("TRN2", target_bir_lowering=False)

    xT = nc.dram_tensor("xT", [D, T], F32, kind="ExternalInput")
    gup = nc.dram_tensor("gup", [E, NPAIR, 128, 2048], F32, kind="ExternalInput")
    dwn = nc.dram_tensor("dwn", [E, NMD, 128, HID], F32, kind="ExternalInput")
    wqk = nc.dram_tensor("wqk", [3, H, 128, 1024], F32, kind="ExternalInput")
    wo = nc.dram_tensor("wo", [NMD, 128, 1024], F32, kind="ExternalInput")
    rtr = nc.dram_tensor("rtr", [128, DC * E], F32, kind="ExternalInput")
    nwc = nc.dram_tensor("nwc", [128, DC], F32, kind="ExternalInput")
    hsr = nc.dram_tensor("hsr", [1, H], F32, kind="ExternalInput")
    alr = nc.dram_tensor("alr", [1, 1], F32, kind="ExternalInput")
    dirs_t = nc.dram_tensor("dirs_t", [128, H], BF16, kind="ExternalInput")
    id_bf = nc.dram_tensor("id_bf", [128, 128], BF16, kind="ExternalInput")
    id_f32 = nc.dram_tensor("id_f32", [8, 8], F32, kind="ExternalInput")
    ones_r = nc.dram_tensor("ones_r", [1, 128], F32, kind="ExternalInput")
    halves_r = nc.dram_tensor("halves_r", [1, 128], F32, kind="ExternalInput")
    ones_rb = nc.dram_tensor("ones_rb", [1, 128], BF16, kind="ExternalInput")
    ones_c = nc.dram_tensor("ones_c", [128, 1], F32, kind="ExternalInput")
    ones_cb = nc.dram_tensor("ones_cb", [128, 1], BF16, kind="ExternalInput")
    cmask = nc.dram_tensor("cmask", [128, 128], BF16, kind="ExternalInput")
    cst = nc.dram_tensor("cst", [128, 8], F32, kind="ExternalInput")
    yout = nc.dram_tensor("yout", [D, T], F32, kind="ExternalOutput")

    es = ExitStack()
    with es:
        s_pe = es.enter_context(nc.semaphore("s_pe"))
        s_act = es.enter_context(nc.semaphore("s_act"))
        s_dve = es.enter_context(nc.semaphore("s_dve"))
        s_ds = es.enter_context(nc.semaphore("s_ds"))
        s_dma = [es.enter_context(nc.semaphore(f"s_dma{i}")) for i in range(10)]
        PE, SC, VE = Ctr(s_pe), Ctr(s_act), Ctr(s_dve)
        DS = Ctr(s_ds, 16)
        DWS = [Ctr(s, 16) for s in s_dma]

        sb = lambda name, shape, dt: es.enter_context(nc.sbuf_tensor(name, shape, dt))
        Xs = sb("X", [128, DC * T], F32)
        Ys = sb("Y", [128, DC * T], BF16)
        nw_sb = sb("nw_sb", [128, DC], F32)
        rtr_sb = sb("rtr_sb", [128, DC * E], F32)
        rtr2_sb = rtr_sb
        hs_sb = sb("hs_sb", [1, H], F32)
        hs2_sb = sb("hs2_sb", [1, H], F32)
        al_sb = sb("al_sb", [1, 1], F32)
        ha_sb = sb("ha_sb", [128, 1], F32)
        dirs_sb = sb("dirs_sb", [128, H], BF16)
        idb_sb = sb("idb_sb", [128, 128], BF16)
        idf_sb = sb("idf_sb", [8, 8], F32)
        ones_r_sb = sb("ones_r_sb", [1, 128], F32)
        halves_r_sb = sb("halves_r_sb", [1, 128], F32)
        ones_rb_sb = sb("ones_rb_sb", [1, 128], BF16)
        ones_c_sb = sb("ones_c_sb", [128, 1], F32)
        ones_cb_sb = sb("ones_cb_sb", [128, 1], BF16)
        cmask_sb = sb("cmask_sb", [128, 128], BF16)
        cst_sb = sb("cst_sb", [128, 8], F32)
        rows = sb("rows", [1, 5 * TB], F32)

        def Xc(c, lo=0, hi=None):
            hi = T if hi is None else hi
            return Xs[:, c * T + lo: c * T + hi]

        def Yc(c, lo=0, hi=None):
            hi = T if hi is None else hi
            return Ys[:, c * T + lo: c * T + hi]

        R_M, R_V, R_RSRG, R_MRSRG, R_A1 = range(5)
        R_A2 = R_V
        C_1PE2, C_E1E2, C_NHALF, C_E1, C_ISQH, C_HALF = range(6)

        def cc(i):
            return cst_sb[:, i:i + 1]

        def cr(i):
            return cst_sb[0:1, i:i + 1]

        def row(i, w=TB):
            return rows[0:1, i * TB: i * TB + w]

        def bank_pad(n, dt):
            q = 512 if dt == F32 else 1024
            return ((n + q - 1) // q) * q

        def psum(stack, name, shape, dt):
            padded = [shape[0], bank_pad(shape[1], dt)]
            t = stack.enter_context(nc.psum_tensor(name, padded, dt))
            return t

        S = Sched()
        loads = [
            (Xs[:].rearrange("p (c t) -> p c t", c=DC),
             xT.ap().rearrange("(c p) t -> p c t", p=128)),
            (nw_sb[:], nwc.ap()), (rtr_sb[:], rtr.ap()), (hs_sb[:], hsr.ap()),
            (al_sb[:], alr.ap()), (dirs_sb[:], dirs_t.ap()), (idb_sb[:], id_bf.ap()),
            (idf_sb[:], id_f32.ap()), (ones_r_sb[:], ones_r.ap()),
            (halves_r_sb[:], halves_r.ap()),
            (ones_rb_sb[:], ones_rb.ap()), (ones_c_sb[:], ones_c.ap()),
            (ones_cb_sb[:], ones_cb.ap()), (cmask_sb[:], cmask.ap()),
            (cst_sb[:], cst.ap()),
        ]
        for dst, src in loads:
            S.op("sync", lambda e, d=dst, s=src: e.dma_start(d, s), DS)
        S.wait("vector", DS, DS.n)
        S.op("vector", lambda e: e.tensor_scalar(hs2_sb[:], hs_sb[:], math.sqrt(HD), None, ALU.mult), VE)
        S.op("vector", lambda e: e.tensor_scalar(al_sb[:], al_sb[:], -0.5, 2.0, ALU.max, ALU.min), VE)
        S.drain("vector")
        S.op("vector", lambda e: e.tensor_scalar(al_sb[:], al_sb[:], 0.5, None, ALU.mult), VE)
        S.drain("vector")
        for c in range(DC):
            S.op("vector", lambda e, c=c: e.tensor_scalar(
                rtr2_sb[:, c * E:(c + 1) * E], rtr_sb[:, c * E:(c + 1) * E],
                nw_sb[:, c:c + 1], None, ALU.mult), VE)
        t_pre_v = VE.n
        with nc.psum_tensor("p_ha", [128, 512], F32) as p_ha_full:
            p_ha = p_ha_full[:, 0:1]
            S.wait("tensor", VE, t_pre_v)
            t_ha = S.op("tensor", lambda e: e.matmul(p_ha, ones_r_sb[:], al_sb[:],
                                                     start=True, stop=True), PE)
            S.wait("vector", PE, t_ha)
            S.op("vector", lambda e: e.tensor_copy(ha_sb[:], p_ha), VE)
            S.emit(nc, "prologue")

        def norm_stats_rows(S, base, pS1, pS2, sq_ring, want_rms, guard):
            S.waitt("tensor", guard.get("ps1"))
            for c in range(DC):
                S.op("tensor", lambda e, c=c: e.matmul(
                    pS1, ones_c_sb[:], Xc(c, base, base + TB),
                    start=(c == 0), stop=(c == DC - 1)), PE if c == DC - 1 else None)
            t_s1 = PE.n
            S.waitt("tensor", guard.get("ps2"))
            t_sqmm = {}
            for c in range(DC):
                sq = sq_ring[c % 2]
                if c >= 2:
                    S.wait("scalar", PE, t_sqmm[c - 2])
                t_sq = S.op("scalar", lambda e, c=c, sq=sq: e.activation(
                    sq[:], Xc(c, base, base + TB), AF.Square), SC)
                S.wait("tensor", SC, t_sq)
                t_sqmm[c] = S.op("tensor", lambda e, sq=sq, c=c: e.matmul(
                    pS2, ones_cb_sb[:], sq[:],
                    start=(c == 0), stop=(c == DC - 1)), PE)
            t_s2 = PE.n
            S.wait("vector", PE, t_s1)
            S.op("vector", lambda e: e.tensor_scalar(row(R_M), pS1, 1.0 / D, None, ALU.mult), VE)
            S.wait("vector", PE, t_s2)
            t_ps2r = S.op("vector", lambda e: e.tensor_scalar(row(R_V), pS2, 1.0 / D, None, ALU.mult), VE)
            guard["ps1"] = (VE, t_ps2r)
            guard["ps2"] = (VE, t_ps2r)
            S.drain("vector")
            S.op("vector", lambda e: e.tensor_tensor(row(R_A1), row(R_M), row(R_M), ALU.mult), VE)
            S.drain("vector")
            t_v = S.op("vector", lambda e: e.tensor_tensor(row(R_V), row(R_V), row(R_A1), ALU.subtract), VE)
            S.wait("scalar", VE, t_v)
            if want_rms:
                S.op("scalar", lambda e: e.activation(row(R_A1), row(R_V), AF.Ln,
                                                      scale=cr(C_1PE2), bias=cr(C_E1E2)), SC)
                S.drain("scalar")
                S.op("scalar", lambda e: e.activation(row(R_RSRG), row(R_A1), AF.Exp, scale=cr(C_NHALF)), SC)
                S.drain("scalar")
                t_ln2r = S.op("scalar", lambda e: e.activation(row(R_MRSRG), row(R_V), AF.Ln, bias=cr(C_E1)), SC)
                S.wait("vector", SC, t_ln2r)
                t_sub = S.op("vector", lambda e: e.tensor_tensor(row(R_A2), row(R_A1), row(R_MRSRG), ALU.subtract), VE)
                S.wait("scalar", VE, t_sub)
                t_invrg = S.op("scalar", lambda e: e.activation(row(R_A1), row(R_A2), AF.Exp, scale=cr(C_HALF)), SC)
                S.wait("vector", SC, t_invrg)
                S.op("vector", lambda e: e.tensor_tensor(row(R_MRSRG), row(R_M), row(R_RSRG), ALU.mult), VE)
            else:
                S.op("scalar", lambda e: e.activation(row(R_A1), row(R_V), AF.Ln, bias=cr(C_E1)), SC)
                S.drain("scalar")
                t_rows_s = S.op("scalar", lambda e: e.activation(row(R_RSRG), row(R_A1), AF.Exp, scale=cr(C_NHALF)), SC)
                S.wait("vector", SC, t_rows_s)
                S.op("vector", lambda e: e.tensor_tensor(row(R_MRSRG), row(R_M), row(R_RSRG), ALU.mult), VE)
            S.drain("vector")
            return VE.n

        def ffn_sublayer(idx):
            S = Sched()
            sbl = ExitStack()
            mk = lambda name, shape, dt: sbl.enter_context(nc.sbuf_tensor(f"{name}_{idx}", shape, dt))
            sq_ring = [mk(f"sq{i}", [128, TB], BF16) for i in range(2)]
            y32_ring = [mk(f"y32_{i}", [128, TB], F32) for i in range(2)]
            tmp2_ring = [mk(f"tmp2_{i}", [128, TB], F32) for i in range(2)]
            Lsb = mk("Lsb", [8, TB], F32)
            Ltok = mk("Ltok", [128, QT * 8], F32)
            Msk = mk("Msk", [128, QT * 8], F32)
            M1 = mk("M1", [128, QT * 8], F32)
            M2 = mk("M2", [128, QT * 8], F32)
            gmax1 = mk("gmax1", [128, QT], F32)
            gmax2 = mk("gmax2", [128, QT], F32)
            gw = mk("gw", [128, 2 * QT], F32)
            C_tok = mk("C_tok", [128, QT * 8], BF16)
            Cexp = mk("Cexp", [8, T], BF16)
            cbrow_t = mk("cbrow", [1, 2 * T], BF16)
            cbrow = [cbrow_t[0:1, 0:T], cbrow_t[0:1, T:2 * T]]
            cb_sb = [mk(f"cb{i}", [128, T], BF16) for i in range(2)]
            gu_ring = [mk(f"gu{i}", [128, 2048], BF16) for i in range(NGU)]
            dn_ring = [mk(f"dn{i}", [128, 8 * 128], BF16) for i in range(NDN)]
            hid_sb = [mk(f"hid{i}", [128, T], BF16) for i in range(max(GROUPS))]
            sg_ring = [mk(f"sg{i}", [128, TH], BF16) for i in range(2)]
            tmpd_ring = [mk(f"tmpd{i}", [128, SW], F32) for i in range(2)]

            ps = ExitStack()
            pS1 = psum(ps, f"pS1_{idx}", [1, TB], F32)[0:1, 0:TB]
            pS2 = psum(ps, f"pS2_{idx}", [1, TB], F32)[0:1, 0:TB]
            pLg = psum(ps, f"pLg_{idx}", [8, TB], F32)[0:8, 0:TB]
            pB1 = psum(ps, f"pB1_{idx}", [128, TB], F32)[:, 0:TB]
            pB2 = psum(ps, f"pB2_{idx}", [128, TB], F32)[:, 0:TB]
            pB3 = psum(ps, f"pB3_{idx}", [128, TB], F32)[:, 0:TB]
            pLt = psum(ps, f"pLt_{idx}", [128, 8], F32)[:, 0:8]
            pCt = psum(ps, f"pCt_{idx}", [8, 128], BF16)[0:8, 0:128]

            guard = {}
            t_y_last = 0
            t_y_nb = {}
            t_ltok = 0
            for nb in range(NB):
                base = nb * TB
                t_rows = norm_stats_rows(S, base, pS1, pS2, sq_ring, True, guard)
                S.wait("tensor", VE, t_rows)
                S.waitt("tensor", guard.get("pB"))
                t_b1 = S.op("tensor", lambda e: e.matmul(pB1, ones_r_sb[:], row(R_RSRG), start=True, stop=True), PE)
                t_b2 = S.op("tensor", lambda e: e.matmul(pB2, ones_r_sb[:], row(R_MRSRG), start=True, stop=True), PE)
                t_b3 = S.op("tensor", lambda e: e.matmul(pB3, halves_r_sb[:], row(R_A1), start=True, stop=True), PE)
                S.waitt("tensor", guard.get("pLg"))
                t_router = 0
                lo, hi = base, base + TB
                for cp in range(0, DC, 2):
                    cs = (cp, cp + 1)
                    for c in cs:
                        S.waitt("vector", guard.get(("y32", c % 2)))
                        S.waitt("vector", guard.get(("y32pe", c % 2)))
                    S.wait("vector", PE, t_b2)
                    S.drain("vector")
                    for c in cs:
                        S.op("vector", lambda e, c=c, lo=lo, hi=hi: e.tensor_tensor(
                            y32_ring[c % 2][:], Xc(c, lo, hi), pB1, ALU.mult), VE)
                    S.drain("vector")
                    for c in cs:
                        t_y32 = S.op("vector", lambda e, c=c: e.tensor_tensor(
                            y32_ring[c % 2][:], y32_ring[c % 2][:], pB2, ALU.subtract), VE)
                    S.wait("scalar", VE, t_y32)
                    for c in cs:
                        t_y_last = S.op("scalar", lambda e, c=c, lo=lo, hi=hi: e.activation(
                            Yc(c, lo, hi), y32_ring[c % 2][:], AF.Copy,
                            scale=nw_sb[:, c:c + 1]), SC)
                        guard[("y32", c % 2)] = (SC, t_y_last)
                    S.wait("tensor", VE, t_y32)
                    for c in cs:
                        t_router = S.op("tensor", lambda e, c=c: e.matmul(
                            pLg, rtr2_sb[:, c * E:(c + 1) * E], y32_ring[c % 2][:],
                            start=(c == 0), stop=(c == DC - 1)), PE)
                        guard[("y32pe", c % 2)] = (PE, t_router)
                    S.wait("vector", PE, t_b3)
                    S.drain("vector")
                    for c in cs:
                        S.op("vector", lambda e, c=c: e.tensor_tensor(
                            tmp2_ring[c % 2][:], y32_ring[c % 2][:], pB3, ALU.mult), VE)
                    S.drain("vector")
                    for c in cs:
                        S.op("vector", lambda e, c=c, lo=lo, hi=hi: e.tensor_tensor(
                            Xc(c, lo, hi), Xc(c, lo, hi), tmp2_ring[c % 2][:], ALU.add), VE)
                guard["pB"] = (VE, VE.n)
                t_y_nb[nb] = t_y_last
                S.wait("scalar", PE, t_router)
                t_lsb = S.op("scalar", lambda e: e.copy(Lsb[:], pLg), SC)
                guard["pLg"] = (SC, t_lsb)
                for j in range(TB // 128):
                    S.wait("tensor", SC, t_lsb)
                    if t_ltok:
                        S.wait("tensor", VE, t_ltok)
                    t_lt = S.op("tensor", lambda e, j=j: e.transpose(
                        pLt, Lsb[:, j * 128:(j + 1) * 128], idf_sb[:]), PE)
                    S.wait("vector", PE, t_lt)
                    q = nb * (TB // 128) + j
                    t_ltok = S.op("vector", lambda e, q=q: e.tensor_copy(
                        Ltok[:, q * 8:(q + 1) * 8], pLt), VE)

            L3 = Ltok[:].rearrange("p (g o) -> p g o", o=8)
            M3 = lambda t: t[:].rearrange("p (g o) -> p g o", o=8)
            g1b = gmax1[:].rearrange("p (g o) -> p g o", o=1).broadcast_to([128, QT, 8])
            g2b = gmax2[:].rearrange("p (g o) -> p g o", o=1).broadcast_to([128, QT, 8])
            S.drain("vector")
            S.op("vector", lambda e: e.reduce_max(
                gmax1[:].rearrange("p (g o) -> p g o", o=1), L3, axis=AX.X), VE)
            S.drain("vector")
            S.op("vector", lambda e: e.tensor_tensor(M3(M1), L3, g1b, ALU.is_ge), VE)
            S.drain("vector")
            S.op("vector", lambda e: e.scalar_tensor_tensor(
                Msk[:], M1[:], NEG_BIG, Ltok[:], ALU.mult, ALU.add), VE)
            S.drain("vector")
            S.op("vector", lambda e: e.reduce_max(
                gmax2[:].rearrange("p (g o) -> p g o", o=1), M3(Msk), axis=AX.X), VE)
            S.drain("vector")
            S.op("vector", lambda e: e.tensor_tensor(M3(M2), M3(Msk), g2b, ALU.is_ge), VE)
            t_d21 = S.op("vector", lambda e: e.tensor_tensor(
                gw[:, QT:2 * QT], gmax2[:], gmax1[:], ALU.subtract), VE)
            S.wait("scalar", VE, t_d21)
            t_w2 = S.op("scalar", lambda e: e.activation(
                gw[:, QT:2 * QT], gw[:, QT:2 * QT], AF.Sigmoid), SC)
            S.wait("vector", SC, t_w2)
            S.op("vector", lambda e: e.tensor_scalar(
                gw[:, 0:QT], gw[:, QT:2 * QT], -1.0, 1.0, ALU.mult, ALU.add), VE)
            S.drain("vector")
            w1b = gw[:, 0:QT].rearrange("p (g o) -> p g o", o=1).broadcast_to([128, QT, 8])
            w2b = gw[:, QT:2 * QT].rearrange("p (g o) -> p g o", o=1).broadcast_to([128, QT, 8])
            S.op("vector", lambda e: e.tensor_tensor(M3(M1), M3(M1), w1b, ALU.mult), VE)
            S.op("vector", lambda e: e.tensor_tensor(M3(M2), M3(M2), w2b, ALU.mult), VE)
            S.drain("vector")
            S.op("vector", lambda e: e.tensor_tensor(M1[:], M1[:], M2[:], ALU.add), VE)
            S.drain("vector")
            t_ctok = S.op("vector", lambda e: e.tensor_scalar(
                C_tok[:], M1[:], ha_sb[:], None, ALU.mult), VE)
            t_cexp = 0
            for q in range(QT):
                S.wait("tensor", VE, t_ctok)
                if t_cexp:
                    S.wait("tensor", VE, t_cexp)
                t_ct = S.op("tensor", lambda e, q=q: e.transpose(
                    pCt, C_tok[:, q * 8:(q + 1) * 8], idb_sb[:]), PE)
                S.wait("vector", PE, t_ct)
                t_cexp = S.op("vector", lambda e, q=q: e.tensor_copy(
                    Cexp[:, q * 128:(q + 1) * 128], pCt), VE)

            ps.close()
            ps2 = ExitStack()
            GU = [psum(ps2, f"pGU{i}_{idx}", [128, TH], F32)[:, 0:TH] for i in range(4)]
            gu_fr = {i: None for i in range(4)}
            slot_gu = {}
            slot_dn = {}
            cb_fr = {}
            cbrow_fr = {}
            t_cb_ready = {}
            t_bcast_prev = 0
            for e_i in range(E):
                sl = e_i % 2
                S.wait("sync", VE, t_cexp)
                S.wait("sync", PE, t_bcast_prev)
                t_cbrow = S.op("sync", lambda e, e_i=e_i, sl=sl: e.dma_start(
                    cbrow[sl], Cexp[e_i:e_i + 1, :]), DS)

                def cb_broadcast(sl=sl, t_cbrow=t_cbrow):
                    for j in range(T // SW):
                        S.wait("tensor", DS, t_cbrow)
                        S.waitt("tensor", gu_fr[0])
                        S.waitt("tensor", cb_fr.get(sl))
                        t_bc = S.op("tensor", lambda e, j=j, sl=sl: e.matmul(
                            GU[0][:, 0:SW], ones_rb_sb[:], cbrow[sl][:, j * SW:(j + 1) * SW],
                            start=True, stop=True), PE)
                        S.wait("scalar", PE, t_bc)
                        t_cbj = S.op("scalar", lambda e, j=j, sl=sl: e.copy(
                            cb_sb[sl][:, j * SW:(j + 1) * SW], GU[0][:, 0:SW]), SC)
                        gu_fr[0] = (SC, t_cbj)
                    cbrow_fr[sl] = (PE, PE.n)
                    t_cb_ready[sl] = SC.n

                if e_i == 0:
                    S.wait("tensor", SC, t_y_last)
                    S.wait("tensor", VE, t_cexp)

                cb_done = False
                k0 = 0
                for g_i, gsz in enumerate(GROUPS):
                    t_hid = {}
                    for i_l in range(gsz):
                        mm_i = k0 + i_l
                        slot = (e_i * NPAIR + mm_i) % NGU
                        warm = (e_i == 0 and mm_i < 4)
                        S.waitt("gpsimd", slot_gu.get(slot))
                        t_w = S.op("gpsimd", lambda e, e_i=e_i, mm_i=mm_i, slot=slot: e.dma_start(
                            gu_ring[slot][:], gup.ap()[e_i, mm_i]), DWS[slot])
                        t_gate_nh = {}
                        t_up_nh = {}
                        S.wait("tensor", DWS[slot], t_w)
                        if warm:
                            for nh in range(NH):
                                S.waitt("tensor", gu_fr[nh])
                                for ns in range(NS):
                                    S.wait("tensor", SC, t_y_nb[(nh * TH + ns * SW) // TB])
                                    for k in range(DC):
                                        S.op("tensor", lambda e, k=k, ns=ns, nh=nh, slot=slot: e.matmul(
                                            GU[nh][:, ns * SW:(ns + 1) * SW],
                                            gu_ring[slot][:, k * 256: k * 256 + 128],
                                            Yc(k, nh * TH + ns * SW, nh * TH + (ns + 1) * SW),
                                            start=(k == 0), stop=(k == DC - 1)),
                                            PE if (k == DC - 1 and ns == NS - 1) else None)
                                t_gate_nh[nh] = PE.n
                            for nh in range(NH):
                                S.waitt("tensor", gu_fr[2 + nh])
                                for ns in range(NS):
                                    for k in range(DC):
                                        S.op("tensor", lambda e, k=k, ns=ns, nh=nh, slot=slot: e.matmul(
                                            GU[2 + nh][:, ns * SW:(ns + 1) * SW],
                                            gu_ring[slot][:, k * 256 + 128: k * 256 + 256],
                                            Yc(k, nh * TH + ns * SW, nh * TH + (ns + 1) * SW),
                                            start=(k == 0), stop=(k == DC - 1)),
                                            PE if (k == DC - 1 and ns == NS - 1) else None)
                                t_up_nh[nh] = PE.n
                        else:
                            for nh in range(NH):
                                S.waitt("tensor", gu_fr[nh])
                            S.wait("tensor", SC, t_y_nb[NB - 1])
                            for k in range(DC):
                                for nh in range(NH):
                                    for ns in range(NS):
                                        S.op("tensor", lambda e, k=k, ns=ns, nh=nh, slot=slot: e.matmul(
                                            GU[nh][:, ns * SW:(ns + 1) * SW],
                                            gu_ring[slot][:, k * 256: k * 256 + 128],
                                            Yc(k, nh * TH + ns * SW, nh * TH + (ns + 1) * SW),
                                            start=(k == 0), stop=(k == DC - 1)),
                                            PE if (k == DC - 1 and ns == NS - 1) else None)
                                        if k == DC - 1 and ns == NS - 1:
                                            t_gate_nh[nh] = PE.n
                            for nh in range(NH):
                                S.waitt("tensor", gu_fr[2 + nh])
                            for k in range(DC):
                                for nh in range(NH):
                                    for ns in range(NS):
                                        S.op("tensor", lambda e, k=k, ns=ns, nh=nh, slot=slot: e.matmul(
                                            GU[2 + nh][:, ns * SW:(ns + 1) * SW],
                                            gu_ring[slot][:, k * 256 + 128: k * 256 + 256],
                                            Yc(k, nh * TH + ns * SW, nh * TH + (ns + 1) * SW),
                                            start=(k == 0), stop=(k == DC - 1)),
                                            PE if (k == DC - 1 and ns == NS - 1) else None)
                                        if k == DC - 1 and ns == NS - 1:
                                            t_up_nh[nh] = PE.n
                        t_pair = PE.n
                        slot_gu[slot] = (PE, t_pair)
                        for nh in range(NH):
                            sg = sg_ring[nh]
                            S.wait("scalar", PE, t_gate_nh[nh])
                            S.waitt("scalar", guard.get(("sg", nh)))
                            t_sg = S.op("scalar", lambda e, nh=nh, sg=sg: e.activation(
                                sg[:], GU[nh], AF.Silu), SC)
                            gu_fr[nh] = (SC, t_sg)
                            S.wait("vector", SC, t_sg)
                            S.wait("vector", PE, t_up_nh[nh])
                            t_h = S.op("vector", lambda e, i_l=i_l, nh=nh, sg=sg: e.tensor_tensor(
                                hid_sb[i_l][:, nh * TH:(nh + 1) * TH], sg[:], GU[2 + nh], ALU.mult), VE)
                            t_hid[(i_l, nh)] = t_h
                            gu_fr[2 + nh] = (VE, t_h)
                            guard[("sg", nh)] = (VE, t_h)
                    if not cb_done:
                        cb_broadcast()
                        t_bcast_prev = PE.n
                        cb_done = True
                    for md in range(NMD):
                        slot = (e_i * len(GROUPS) * NMD + g_i * NMD + md) % NDN
                        S.waitt("gpsimd", slot_dn.get(slot))
                        t_wd = S.op("gpsimd", lambda e, e_i=e_i, md=md, k0=k0, gsz=gsz, slot=slot: e.dma_start(
                            dn_ring[slot][:, 0:gsz * 128],
                            dwn.ap()[e_i, md][:, k0 * 128:(k0 + gsz) * 128]), DWS[NGU + slot])
                        dpar = (md % 2) * 2
                        for nh in range(NH):
                            S.waitt("tensor", gu_fr[dpar + nh])
                        S.wait("tensor", DWS[NGU + slot], t_wd)
                        for k in range(gsz):
                            lastk = (k == gsz - 1)
                            S.wait("tensor", VE, t_hid[(k, NH - 1)])
                            for nh in range(NH):
                                for ns in range(NS):
                                    S.op("tensor", lambda e, k=k, ns=ns, nh=nh, slot=slot, dpar=dpar, gsz=gsz: e.matmul(
                                        GU[dpar + nh][:, ns * SW:(ns + 1) * SW],
                                        dn_ring[slot][:, k * 128:(k + 1) * 128],
                                        hid_sb[k][:, nh * TH + ns * SW: nh * TH + (ns + 1) * SW],
                                        start=(k == 0), stop=(k == gsz - 1)),
                                        PE if (lastk and nh == NH - 1 and ns == NS - 1) else None)
                        t_dmm = PE.n
                        slot_dn[slot] = (PE, t_dmm)
                        S.wait("vector", PE, t_dmm)
                        S.wait("vector", SC, t_cb_ready[sl])
                        t_td = 0
                        for nh in range(NH):
                            for ns in range(NS):
                                td = tmpd_ring[ns % 2]
                                S.drain("vector")
                                S.op("vector", lambda e, td=td, ns=ns, nh=nh, dpar=dpar, sl=sl: e.tensor_tensor(
                                    td[:], GU[dpar + nh][:, ns * SW:(ns + 1) * SW],
                                    cb_sb[sl][:, nh * TH + ns * SW: nh * TH + (ns + 1) * SW],
                                    ALU.mult), VE)
                                S.drain("vector")
                                t_td = S.op("vector", lambda e, td=td, ns=ns, nh=nh, md=md: e.tensor_tensor(
                                    Xc(md, nh * TH + ns * SW, nh * TH + (ns + 1) * SW),
                                    Xc(md, nh * TH + ns * SW, nh * TH + (ns + 1) * SW),
                                    td[:], ALU.add), VE)
                        gu_fr[dpar] = (VE, t_td)
                        gu_fr[dpar + 1] = (VE, t_td)
                        cb_fr[sl] = (VE, t_td)
                    k0 += gsz

            S.emit(nc, f"ffn{idx}")
            ps2.close()
            sbl.close()

        def attn_sublayer():
            S = Sched()
            sbl = ExitStack()
            mk = lambda name, shape, dt: sbl.enter_context(nc.sbuf_tensor(f"{name}_a", shape, dt))
            sq_ring = [mk(f"sq{i}", [128, TB], BF16) for i in range(2)]
            a32_ring = [mk("a32", [128, TB], F32)] * 2
            v_sb = mk("v_sb", [128, QT * 1024], BF16)
            v_fm = mk("v_fm", [128, T], BF16)
            fm = mk("fm", [128, H * T], BF16)
            q_sb = mk("q_sb", [128, T], BF16)
            k_sb = mk("k_sb", [128, T], BF16)
            p_sb = mk("p_sb", [128, T], BF16)
            pT_sb = [mk(f"pT{i}", [128, 128], BF16) for i in range(3)]
            o_tok = mk("o_tok", [128, 128], BF16)
            lqlk = mk("lqlk", [1, 2 * T], BF16)
            lqr = lqlk[0:1, 0:T]
            lkr = lqlk[0:1, T:2 * T]
            nmax = mk("nmax", [128, 1], F32)
            bmax = mk("bmax", [128, 4], F32)
            rsum = mk("rsum", [128, 2], F32)
            rcp = mk("rcp", [128, 1], F32)
            wqk_ring = [mk(f"wqk{i}", [128, 1024], BF16) for i in range(3)]
            wo_ring = wqk_ring

            psn = ExitStack()
            pS1 = psum(psn, "pS1_a", [1, TB], F32)[0:1, 0:TB]
            pS2 = psum(psn, "pS2_a", [1, TB], F32)[0:1, 0:TB]
            pB1 = psum(psn, "pB1_a", [128, TB], F32)[:, 0:TB]
            pB2 = psum(psn, "pB2_a", [128, TB], F32)[:, 0:TB]
            guard = {}
            for nb in range(NB):
                base = nb * TB
                t_rows = norm_stats_rows(S, base, pS1, pS2, sq_ring, False, guard)
                S.wait("tensor", VE, t_rows)
                S.waitt("tensor", guard.get("pB"))
                t_b1 = S.op("tensor", lambda e: e.matmul(pB1, ones_r_sb[:], row(R_RSRG), start=True, stop=True), PE)
                t_b2 = S.op("tensor", lambda e: e.matmul(pB2, ones_r_sb[:], row(R_MRSRG), start=True, stop=True), PE)
                for c in range(DC):
                    a32 = a32_ring[c % 2]
                    lo, hi = base, base + TB
                    S.wait("vector", PE, t_b2)
                    S.drain("vector")
                    S.op("vector", lambda e, c=c, a32=a32, lo=lo, hi=hi: e.tensor_tensor(
                        a32[:], Xc(c, lo, hi), pB1, ALU.mult), VE)
                    S.drain("vector")
                    S.op("vector", lambda e, c=c, a32=a32, lo=lo, hi=hi: e.tensor_tensor(
                        Yc(c, lo, hi), a32[:], pB2, ALU.subtract), VE)
                guard["pB"] = (VE, VE.n)
            t_ln2 = VE.n
            psn.close()

            ps = ExitStack()
            pS = psum(ps, "pS_a", [128, min(2048, T)], F32)
            pQK = psum(ps, "pQK_a", [128, 512], F32)
            pAV = psum(ps, "pAV_a", [128, 128], F32)[:, 0:128]
            pTr2 = [psum(ps, f"pTr{i}_a", [128, 128], BF16)[:, 0:128] for i in range(2)]

            pqk_fr = None
            wqk_fr = {}
            qk_buf_fr = {0: None, 1: None}
            t_ps_read = None
            ps_rd = {0: None, 1: None}
            t_lrow_fr = {}
            t_av_fr = None
            tr_fr = {}
            t_ot_read = None
            t_nmax_fr = None
            t_rsum_fr = None
            t_fm_last = 0
            v_fm_fr = None
            for h in range(H):
                for sel in range(3):
                    S.waitt("gpsimd", wqk_fr.get(sel))
                    t_wq = S.op("gpsimd", lambda e, h=h, sel=sel: e.dma_start(
                        wqk_ring[sel][:], wqk.ap()[sel, h]), DWS[NGU + NDN + sel])
                    dst = [q_sb, k_sb, v_fm][sel]
                    for nh in range(NH):
                        for ns in range(NS):
                            lo = nh * TH + ns * SW
                            S.wait("tensor", DWS[NGU + NDN + sel], t_wq)
                            if h == 0:
                                S.wait("tensor", VE, t_ln2)
                            S.waitt("tensor", pqk_fr)
                            for k in range(DC):
                                S.op("tensor", lambda e, k=k, lo=lo, sel=sel: e.matmul(
                                    pQK[:, 0:SW],
                                    wqk_ring[sel][:, k * 128:(k + 1) * 128],
                                    Yc(k, lo, lo + SW),
                                    start=(k == 0), stop=(k == DC - 1)),
                                    PE if k == DC - 1 else None)
                            t_mm = PE.n
                            if sel == 0:
                                S.wait("scalar", PE, t_mm)
                                S.waitt("scalar", qk_buf_fr[0])
                                t_ev = S.op("scalar", lambda e, lo=lo: e.activation(
                                    q_sb[:, lo:lo + SW], pQK[:, 0:SW], AF.Copy,
                                    scale=cc(C_ISQH)), SC)
                                pqk_fr = (SC, t_ev)
                            else:
                                S.wait("vector", PE, t_mm)
                                S.waitt("vector", qk_buf_fr[1] if sel == 1 else v_fm_fr)
                                t_ev = S.op("vector", lambda e, lo=lo, dst=dst: e.tensor_copy(
                                    dst[:, lo:lo + SW], pQK[:, 0:SW]), VE)
                                pqk_fr = (VE, t_ev)
                    wqk_fr[sel] = (PE, PE.n)
                t_vfm_ev = VE.n
                for tt in range(QT):
                    ptr = pTr2[tt % 2]
                    S.wait("tensor", VE, t_vfm_ev)
                    S.waitt("tensor", tr_fr.get(tt % 2))
                    t_vt = S.op("tensor", lambda e, tt=tt, ptr=ptr: e.transpose(
                        ptr, v_fm[:, tt * 128:(tt + 1) * 128], idb_sb[:]), PE)
                    S.wait("vector", PE, t_vt)
                    t_ev = S.op("vector", lambda e, tt=tt, h=h, ptr=ptr: e.tensor_copy(
                        v_sb[:, tt * 1024 + h * 128: tt * 1024 + (h + 1) * 128], ptr), VE)
                    tr_fr[tt % 2] = (VE, t_ev)
                v_fm_fr = (PE, PE.n)
                t_v_done = VE.n
                t_q_ready = SC.n
                t_k_ready = VE.n
                nrow = T // SW
                t_lrow_s = 0
                t_lrow_v = 0
                for sel in range(2):
                    src = q_sb if sel == 0 else k_sb
                    for j in range(nrow):
                        S.wait("tensor", SC, t_q_ready)
                        S.wait("tensor", VE, t_k_ready)
                        S.waitt("tensor", ps_rd[0])
                        S.waitt("tensor", ps_rd[1])
                        S.waitt("tensor", t_lrow_fr.get((sel ^ 1, j)))
                        S.waitt("tensor", t_lrow_fr.get((sel, j)))
                        t_r = S.op("tensor", lambda e, j=j, src=src, h=h: e.matmul(
                            pS[0:1, j * SW:(j + 1) * SW], dirs_sb[:, h:h + 1],
                            src[:, j * SW:(j + 1) * SW], start=True, stop=True), PE)
                        if sel == 0:
                            S.wait("scalar", PE, t_r)
                            t_lrow_s = S.op("scalar", lambda e, j=j, h=h: e.activation(
                                lqr[:, j * SW:(j + 1) * SW], pS[0:1, j * SW:(j + 1) * SW],
                                AF.Copy, scale=hs2_sb[0:1, h:h + 1]), SC)
                            t_lrow_fr[(sel, j)] = (SC, t_lrow_s)
                        else:
                            S.wait("vector", PE, t_r)
                            t_lrow_v = S.op("vector", lambda e, j=j: e.tensor_copy(
                                lkr[:, j * SW:(j + 1) * SW], pS[0:1, j * SW:(j + 1) * SW]), VE)
                            t_lrow_fr[(sel, j)] = (VE, t_lrow_v)
                ps_rd = {0: t_ps_read, 1: t_ps_read}
                for t_i in range(QT):
                    KT = (t_i + 1) * 128
                    nkb = (KT + SW - 1) // SW
                    if nkb <= 2 and min(2048, T) >= 2048:
                        half = t_i % 2
                        base_b = half * 2
                    else:
                        half = None
                        base_b = 0
                    S.wait("tensor", SC, t_lrow_s)
                    S.wait("tensor", VE, max(t_lrow_v, t_v_done))
                    if half is None:
                        S.waitt("tensor", ps_rd[0])
                        S.waitt("tensor", ps_rd[1])
                    else:
                        S.waitt("tensor", ps_rd[half])
                    for b in range(nkb):
                        w = min(SW, KT - b * SW)
                        off = (base_b + b) * SW
                        S.op("tensor", lambda e, t_i=t_i, b=b, w=w, off=off: e.matmul(
                            pS[:, off: off + w],
                            q_sb[:, t_i * 128:(t_i + 1) * 128],
                            k_sb[:, b * SW: b * SW + w], start=True, stop=False), None)
                    t_geo = {}
                    for b in range(nkb):
                        w = min(SW, KT - b * SW)
                        off = (base_b + b) * SW
                        t_geo[b] = S.op("tensor", lambda e, t_i=t_i, b=b, w=w, off=off: e.matmul(
                            pS[:, off: off + w],
                            lqr[:, t_i * 128:(t_i + 1) * 128],
                            lkr[:, b * SW: b * SW + w], start=False, stop=True), PE)
                    for b in range(nkb):
                        w = min(SW, KT - b * SW)
                        off = (base_b + b) * SW
                        S.wait("vector", PE, t_geo[b])
                        if b == nkb - 1:
                            doff = base_b * SW + t_i * 128
                            S.op("vector", lambda e, doff=doff: e.tensor_tensor(
                                pS[:, doff:doff + 128],
                                pS[:, doff:doff + 128], cmask_sb[:], ALU.add), VE)
                            S.drain("vector")
                        S.op("vector", lambda e, b=b, w=w, off=off: e.reduce_max(
                            bmax[:, b:b + 1], pS[:, off: off + w], axis=AX.X), VE)
                        S.drain("vector")
                    S.waitt("vector", t_nmax_fr)
                    t_nm = S.op("vector", lambda e, nkb=nkb: e.reduce_max(
                        nmax[:], bmax[:, 0:nkb], axis=AX.X, negate=True), VE)
                    S.wait("scalar", VE, t_nm)
                    S.waitt("scalar", guard.get("p_sb"))
                    S.waitt("scalar", t_rsum_fr)
                    KT1 = 128 * ((t_i + 1) // 2) if KT > 256 else KT
                    t_exp1 = S.op("scalar", lambda e, KT1=KT1, base_b=base_b: e.activation(
                        p_sb[:, 0:KT1], pS[:, base_b * SW: base_b * SW + KT1],
                        AF.Exp, bias=nmax[:], accum_out=rsum[0:128, 0:1]), SC)
                    if KT1 < KT:
                        S.drain("scalar")
                        t_exp = S.op("scalar", lambda e, KT1=KT1, KT=KT, base_b=base_b: e.activation(
                            p_sb[:, KT1:KT], pS[:, base_b * SW + KT1: base_b * SW + KT],
                            AF.Exp, bias=nmax[:], accum_out=rsum[0:128, 1:2]), SC)
                    else:
                        t_exp = t_exp1
                    if half is None:
                        ps_rd[0] = (SC, t_exp)
                        ps_rd[1] = (SC, t_exp)
                    else:
                        ps_rd[half] = (SC, t_exp)
                    t_nmax_fr = (SC, t_exp)
                    S.wait("vector", SC, t_exp)
                    if KT1 < KT:
                        S.drain("vector")
                        t_sum = S.op("vector", lambda e: e.tensor_tensor(
                            rsum[0:128, 0:1], rsum[0:128, 0:1], rsum[0:128, 1:2], ALU.add), VE)
                        S.drain("vector")
                    t_rcp = S.op("vector", lambda e: e.reciprocal(rcp[:], rsum[0:128, 0:1]), VE)
                    t_rsum_fr = (VE, t_rcp)
                    pt_fr = {}
                    pt_ev = {}

                    def emit_av(b, t_i=t_i, h=h):
                        S.waitt("tensor", pt_ev[b])
                        if b == 0:
                            S.waitt("tensor", t_av_fr)
                        S.op("tensor", lambda e, b=b, t_i=t_i, h=h: e.matmul(
                            pAV, pT_sb[b % 3][:],
                            v_sb[:, b * 1024 + h * 128: b * 1024 + (h + 1) * 128],
                            start=(b == 0), stop=(b == t_i)),
                            PE if b == t_i else None)
                        pt_fr[b] = (PE, PE.n)

                    for b in range(t_i + 1):
                        pts = pT_sb[b % 3]
                        ptr = pTr2[b % 2]
                        S.wait("tensor", SC, t_exp)
                        S.waitt("tensor", tr_fr.get(b % 2))
                        t_trm = S.op("tensor", lambda e, b=b, ptr=ptr: e.transpose(
                            ptr, p_sb[:, b * 128:(b + 1) * 128], idb_sb[:]), PE)
                        S.wait("vector", PE, t_trm)
                        S.waitt("vector", pt_fr.get(b - 3))
                        t_ev = S.op("vector", lambda e, pts=pts, ptr=ptr: e.tensor_copy(
                            pts[:], ptr), VE)
                        tr_fr[b % 2] = (VE, t_ev)
                        pt_ev[b] = (VE, t_ev)
                        if b >= 2:
                            emit_av(b - 2)
                    for b in range(max(0, t_i - 1), t_i + 1):
                        emit_av(b)
                    t_avmm = PE.n
                    guard["p_sb"] = (PE, t_avmm)
                    S.wait("vector", PE, t_avmm)
                    S.wait("vector", VE, t_rcp)
                    S.drain("vector")
                    S.waitt("vector", t_ot_read)
                    t_ot = S.op("vector", lambda e: e.tensor_scalar(
                        o_tok[:], pAV, rcp[:], None, ALU.mult), VE)
                    t_av_fr = (VE, t_ot)
                    ptro = pTr2[(t_i + 1) % 2]
                    S.wait("tensor", VE, t_ot)
                    S.waitt("tensor", tr_fr.get((t_i + 1) % 2))
                    t_otr = S.op("tensor", lambda e, ptro=ptro: e.transpose(ptro, o_tok[:], idb_sb[:]), PE)
                    t_ot_read = (PE, t_otr)
                    S.wait("vector", PE, t_otr)
                    t_fme = S.op("vector", lambda e, h=h, t_i=t_i, ptro=ptro: e.tensor_copy(
                        fm[:, h * T + t_i * 128: h * T + (t_i + 1) * 128], ptro), VE)
                    tr_fr[(t_i + 1) % 2] = (VE, t_fme)
                    t_fm_last = t_fme
                qk_buf_fr[0] = (PE, PE.n)
                qk_buf_fr[1] = (PE, PE.n)

            wo_fr = {0: wqk_fr.get(0), 1: wqk_fr.get(1)}
            op_fr = {0: ps_rd[0], 1: ps_rd[1]}
            for md in range(NMD):
                slot = md % 2
                S.waitt("gpsimd", wo_fr.get(slot))
                t_wo = S.op("gpsimd", lambda e, md=md, slot=slot: e.dma_start(
                    wo_ring[slot][:], wo.ap()[md]), DWS[NGU + NDN + slot])
                for nh in range(NH):
                    opar = (md * NH + nh) % 2
                    if 2 * TH <= min(2048, T):
                        dst = pS[:, opar * TH:(opar + 1) * TH]
                    else:
                        dst = pS[:, 0:TH]
                        opar = 0
                    S.wait("tensor", DWS[NGU + NDN + slot], t_wo)
                    S.wait("tensor", VE, t_fm_last)
                    S.waitt("tensor", op_fr[opar])
                    for k in range(DC):
                        last = (k == DC - 1)
                        for ns in range(NS):
                            S.op("tensor", lambda e, k=k, ns=ns, nh=nh, slot=slot, dst=dst: e.matmul(
                                dst[:, ns * SW:(ns + 1) * SW],
                                wo_ring[slot][:, k * 128:(k + 1) * 128],
                                fm[:, k * T + nh * TH + ns * SW: k * T + nh * TH + (ns + 1) * SW],
                                start=(k == 0), stop=(k == DC - 1)),
                                PE if (last and ns == NS - 1) else None)
                    t_omm = PE.n
                    S.wait("vector", PE, t_omm)
                    t_oadd = S.op("vector", lambda e, md=md, nh=nh, dst=dst: e.tensor_tensor(
                        Xc(md, nh * TH, (nh + 1) * TH),
                        Xc(md, nh * TH, (nh + 1) * TH), dst[:, 0:TH], ALU.add), VE)
                    op_fr[opar] = (VE, t_oadd)
                wo_fr[slot] = (PE, PE.n)

            S.emit(nc, "attn")
            ps.close()
            sbl.close()

        if 1 in parts:
            ffn_sublayer(1)
        if 2 in parts:
            attn_sublayer()
        if 3 in parts:
            ffn_sublayer(2)

        S = Sched()
        S.op("sync", lambda e: e.dma_start(
            yout.ap().rearrange("(c p) t -> p c t", p=128),
            Xs[:].rearrange("p (c t) -> p c t", c=DC)), DS)
        S.wait("sync", DS, DS.n)
        S.emit(nc, "out")

    return nc


def prepare_in_maps(inputs, T, n_cores):
    x = np.asarray(inputs["L0_attn_x"], dtype=np.float32)
    w_qkv = np.ascontiguousarray(np.asarray(inputs["L0_attn_w_qkv"], dtype=np.float32))
    w_out = np.asarray(inputs["L0_attn_w_out"], dtype=np.float32)
    head_scales = np.asarray(inputs["L0_attn_head_scales"], dtype=np.float32)
    nw = np.asarray(inputs["L0_ffn_norm_weight"], dtype=np.float32)
    rw = np.asarray(inputs["L0_ffn_router_weight"], dtype=np.float32)
    guw = np.asarray(inputs["L0_ffn_gate_up_weight"], dtype=np.float32)
    dww = np.asarray(inputs["L0_ffn_down_weight"], dtype=np.float32)
    alpha = np.asarray(inputs["L0_ffn_residual_alpha"], dtype=np.float32).reshape(1, 1)

    gup_s = np.empty((E, NPAIR, 128, 2048), np.float32)
    for e in range(E):
        WT = guw[e].T
        G = WT[:, :HID].reshape(DC, 128, NPAIR, 128)
        U = WT[:, HID:].reshape(DC, 128, NPAIR, 128)
        pair = np.concatenate([G, U], axis=3)
        gup_s[e] = pair.transpose(2, 1, 0, 3).reshape(NPAIR, 128, 2048)
    dwn_s = np.empty((E, NMD, 128, HID), np.float32)
    for e in range(E):
        DT = dww[e].T
        A = DT.reshape(NPAIR, 128, NMD, 128)
        dwn_s[e] = A.transpose(2, 1, 0, 3).reshape(NMD, 128, HID)
    Q = w_qkv.reshape(DC, 128, 3, H, 128)
    wqk_s = np.ascontiguousarray(Q.transpose(2, 3, 1, 0, 4).reshape(3, H, 128, 1024))
    O = w_out.reshape(DC, 128, NMD, 128)
    wo_s = np.ascontiguousarray(O.transpose(2, 1, 0, 3).reshape(NMD, 128, 1024))
    rtr_s = np.ascontiguousarray(rw.T.reshape(DC, 128, E).transpose(1, 0, 2).reshape(128, DC * E))
    nw_s = np.ascontiguousarray(nw.reshape(DC, 128).T)
    hs_s = head_scales.reshape(1, H)
    t = np.array([[(i >> 2 & 1) * 2 - 1, (i >> 1 & 1) * 2 - 1, (i & 1) * 2 - 1]
                  for i in range(8)], dtype=np.float32) / np.sqrt(3.0, dtype=np.float32)
    dirs = np.zeros((128, H), np.float32)
    dirs[0:3, :] = t.T
    shared = dict(
        gup=gup_s, dwn=dwn_s, wqk=wqk_s, wo=wo_s, rtr=rtr_s,
        nwc=nw_s, hsr=hs_s, alr=alpha,
        dirs_t=dirs.astype(ml_dtypes.bfloat16),
        id_bf=np.eye(128, dtype=ml_dtypes.bfloat16),
        id_f32=np.eye(8, dtype=np.float32),
        ones_r=np.ones((1, 128), np.float32),
        ones_rb=np.ones((1, 128), ml_dtypes.bfloat16),
        ones_c=np.ones((128, 1), np.float32),
        ones_cb=np.ones((128, 1), ml_dtypes.bfloat16),
        cmask=np.where(np.arange(128)[:, None] >= np.arange(128)[None, :],
                       0.0, NEG_BIG).astype(ml_dtypes.bfloat16),
        cst=np.tile(np.array([[1.0 + EPS_RMS, EPS_LN * EPS_RMS, -0.5, EPS_LN,
                               1.0 / math.sqrt(HD), 0.5, 0.0, 0.0]], np.float32),
                    (128, 1)),
        halves_r=np.full((1, 128), 0.5, np.float32),
    )
    in_maps = []
    for b in range(n_cores):
        m = dict(shared)
        m["xT"] = np.ascontiguousarray(x[b, :T, :].T)
        in_maps.append(m)
    return in_maps


def kernel(**inputs):
    T = 2048
    n_cores = 8
    in_maps = prepare_in_maps(inputs, T, n_cores)
    nc = build_program(T)
    res = run_bass_kernel_spmd(nc, in_maps, core_ids=list(range(n_cores)))
    out = np.empty((n_cores, T, D), np.float32)
    for b in range(n_cores):
        out[b] = res.results[b]["yout"].T
    return out


# version 11
# speedup vs baseline: 1.0648x; 1.0648x over previous
import math
from contextlib import ExitStack

import numpy as np
import ml_dtypes
import concourse.bass as bass
import concourse.mybir as mybir
from concourse.bass_utils import run_bass_kernel_spmd

F32 = mybir.dt.float32
BF16 = mybir.dt.bfloat16
AF = mybir.ActivationFunctionType
ALU = mybir.AluOpType
AX = mybir.AxisListType

D = 1024
DC = D // 128
HID = 2816
E = 8
H = 8
HD = 128
NPAIR = 22
GROUPS = (8, 8, 6)
NMD = 8
EPS_LN = 1e-5
EPS_RMS = 1e-6
NEG_BIG = -1.0e30
NGU = 3
NDN = 2


class Ctr:
    def __init__(self, sem, step=1):
        self.sem = sem
        self.n = 0
        self.step = step


class Sched:
    ENGINES = ("tensor", "scalar", "vector", "gpsimd", "sync")

    def __init__(self):
        self.ops = {e: [] for e in self.ENGINES}
        self.last_wait = {e: {} for e in self.ENGINES}

    def wait(self, eng, ctr, val):
        if val is None or val <= 0:
            return
        lw = self.last_wait[eng]
        k = id(ctr)
        if lw.get(k, 0) >= val:
            return
        lw[k] = val
        self.ops[eng].append(("w", ctr.sem, val))

    def waitt(self, eng, ticket):
        if ticket is not None:
            self.wait(eng, ticket[0], ticket[1])

    def op(self, eng, fn, inc=None):
        self.ops[eng].append(("o", fn, inc))
        if inc is not None:
            inc.n += inc.step
            return inc.n

    def drain(self, eng):
        self.ops[eng].append(("d", None, None))

    def emit(self, nc, name):
        with nc.Block(name) as blk:
            for ename in self.ENGINES:
                ops = self.ops[ename]
                if not ops:
                    continue

                def body(e, ops=ops):
                    for kind, a, b in ops:
                        if kind == "w":
                            e.wait_ge(a, b)
                        elif kind == "d":
                            e.drain()
                        else:
                            inst = a(e)
                            if b is not None:
                                inst.then_inc(b.sem, b.step)
                getattr(blk, ename)(body)


def build_program(T, parts=(1, 2, 3)):
    assert T % 128 == 0 and 128 <= T <= 2048
    QT = T // 128
    TB = min(512, T)
    NB = T // TB
    TH = min(1024, T)
    NH = T // TH
    SW = min(512, TH)
    NS = TH // SW

    nc = bass.Bass("TRN2", target_bir_lowering=False)

    xT = nc.dram_tensor("xT", [D, T], F32, kind="ExternalInput")
    gup = nc.dram_tensor("gup", [E, NPAIR, 128, 2048], F32, kind="ExternalInput")
    dwn = nc.dram_tensor("dwn", [E, NMD, 128, HID], F32, kind="ExternalInput")
    wqk = nc.dram_tensor("wqk", [3, H, 128, 1024], F32, kind="ExternalInput")
    wo = nc.dram_tensor("wo", [NMD, 128, 1024], F32, kind="ExternalInput")
    rtr = nc.dram_tensor("rtr", [128, DC * E], F32, kind="ExternalInput")
    nwc = nc.dram_tensor("nwc", [128, DC], F32, kind="ExternalInput")
    hsr = nc.dram_tensor("hsr", [1, H], F32, kind="ExternalInput")
    alr = nc.dram_tensor("alr", [1, 1], F32, kind="ExternalInput")
    dirs_t = nc.dram_tensor("dirs_t", [128, H], BF16, kind="ExternalInput")
    id_bf = nc.dram_tensor("id_bf", [128, 128], BF16, kind="ExternalInput")
    id_f32 = nc.dram_tensor("id_f32", [8, 8], F32, kind="ExternalInput")
    ones_r = nc.dram_tensor("ones_r", [1, 128], F32, kind="ExternalInput")
    halves_r = nc.dram_tensor("halves_r", [1, 128], F32, kind="ExternalInput")
    ones_rb = nc.dram_tensor("ones_rb", [1, 128], BF16, kind="ExternalInput")
    ones_c = nc.dram_tensor("ones_c", [128, 1], F32, kind="ExternalInput")
    ones_cb = nc.dram_tensor("ones_cb", [128, 1], BF16, kind="ExternalInput")
    cmask = nc.dram_tensor("cmask", [128, 128], BF16, kind="ExternalInput")
    cst = nc.dram_tensor("cst", [128, 8], F32, kind="ExternalInput")
    yout = nc.dram_tensor("yout", [D, T], F32, kind="ExternalOutput")

    es = ExitStack()
    with es:
        s_pe = es.enter_context(nc.semaphore("s_pe"))
        s_act = es.enter_context(nc.semaphore("s_act"))
        s_dve = es.enter_context(nc.semaphore("s_dve"))
        s_ds = es.enter_context(nc.semaphore("s_ds"))
        s_dma = [es.enter_context(nc.semaphore(f"s_dma{i}")) for i in range(10)]
        PE, SC, VE = Ctr(s_pe), Ctr(s_act), Ctr(s_dve)
        DS = Ctr(s_ds, 16)
        DWS = [Ctr(s, 16) for s in s_dma]

        sb = lambda name, shape, dt: es.enter_context(nc.sbuf_tensor(name, shape, dt))
        Xs = sb("X", [128, DC * T], F32)
        Ys = sb("Y", [128, DC * T], BF16)
        nw_sb = sb("nw_sb", [128, DC], F32)
        rtr_sb = sb("rtr_sb", [128, DC * E], F32)
        rtr2_sb = rtr_sb
        hs_sb = sb("hs_sb", [1, H], F32)
        hs2_sb = sb("hs2_sb", [1, H], F32)
        al_sb = sb("al_sb", [1, 1], F32)
        ha_sb = sb("ha_sb", [128, 1], F32)
        dirs_sb = sb("dirs_sb", [128, H], BF16)
        idb_sb = sb("idb_sb", [128, 128], BF16)
        idf_sb = sb("idf_sb", [8, 8], F32)
        ones_r_sb = sb("ones_r_sb", [1, 128], F32)
        halves_r_sb = sb("halves_r_sb", [1, 128], F32)
        ones_rb_sb = sb("ones_rb_sb", [1, 128], BF16)
        ones_c_sb = sb("ones_c_sb", [128, 1], F32)
        ones_cb_sb = sb("ones_cb_sb", [128, 1], BF16)
        cmask_sb = sb("cmask_sb", [128, 128], BF16)
        cst_sb = sb("cst_sb", [128, 8], F32)
        rows = sb("rows", [1, 5 * TB], F32)

        def Xc(c, lo=0, hi=None):
            hi = T if hi is None else hi
            return Xs[:, c * T + lo: c * T + hi]

        def Yc(c, lo=0, hi=None):
            hi = T if hi is None else hi
            return Ys[:, c * T + lo: c * T + hi]

        R_M, R_V, R_RSRG, R_MRSRG, R_A1 = range(5)
        R_A2 = R_V
        C_1PE2, C_E1E2, C_NHALF, C_E1, C_ISQH, C_HALF = range(6)

        def cc(i):
            return cst_sb[:, i:i + 1]

        def cr(i):
            return cst_sb[0:1, i:i + 1]

        def row(i, w=TB):
            return rows[0:1, i * TB: i * TB + w]

        def bank_pad(n, dt):
            q = 512 if dt == F32 else 1024
            return ((n + q - 1) // q) * q

        def psum(stack, name, shape, dt):
            padded = [shape[0], bank_pad(shape[1], dt)]
            t = stack.enter_context(nc.psum_tensor(name, padded, dt))
            return t

        S = Sched()
        loads = [
            (Xs[:].rearrange("p (c t) -> p c t", c=DC),
             xT.ap().rearrange("(c p) t -> p c t", p=128)),
            (nw_sb[:], nwc.ap()), (rtr_sb[:], rtr.ap()), (hs_sb[:], hsr.ap()),
            (al_sb[:], alr.ap()), (dirs_sb[:], dirs_t.ap()), (idb_sb[:], id_bf.ap()),
            (idf_sb[:], id_f32.ap()), (ones_r_sb[:], ones_r.ap()),
            (halves_r_sb[:], halves_r.ap()),
            (ones_rb_sb[:], ones_rb.ap()), (ones_c_sb[:], ones_c.ap()),
            (ones_cb_sb[:], ones_cb.ap()), (cmask_sb[:], cmask.ap()),
            (cst_sb[:], cst.ap()),
        ]
        for dst, src in loads:
            S.op("sync", lambda e, d=dst, s=src: e.dma_start(d, s), DS)
        S.wait("vector", DS, DS.n)
        S.op("vector", lambda e: e.tensor_scalar(hs2_sb[:], hs_sb[:], math.sqrt(HD), None, ALU.mult), VE)
        S.op("vector", lambda e: e.tensor_scalar(al_sb[:], al_sb[:], -0.5, 2.0, ALU.max, ALU.min), VE)
        S.drain("vector")
        S.op("vector", lambda e: e.tensor_scalar(al_sb[:], al_sb[:], 0.5, None, ALU.mult), VE)
        S.drain("vector")
        for c in range(DC):
            S.op("vector", lambda e, c=c: e.tensor_scalar(
                rtr2_sb[:, c * E:(c + 1) * E], rtr_sb[:, c * E:(c + 1) * E],
                nw_sb[:, c:c + 1], None, ALU.mult), VE)
        t_pre_v = VE.n
        with nc.psum_tensor("p_ha", [128, 512], F32) as p_ha_full:
            p_ha = p_ha_full[:, 0:1]
            S.wait("tensor", VE, t_pre_v)
            t_ha = S.op("tensor", lambda e: e.matmul(p_ha, ones_r_sb[:], al_sb[:],
                                                     start=True, stop=True), PE)
            S.wait("vector", PE, t_ha)
            S.op("vector", lambda e: e.tensor_copy(ha_sb[:], p_ha), VE)
            S.emit(nc, "prologue")

        def norm_stats_rows(S, base, pS1, pS2, sq_ring, want_rms, guard):
            S.waitt("tensor", guard.get("ps1"))
            for c in range(DC):
                S.op("tensor", lambda e, c=c: e.matmul(
                    pS1, ones_c_sb[:], Xc(c, base, base + TB),
                    start=(c == 0), stop=(c == DC - 1)), PE if c == DC - 1 else None)
            t_s1 = PE.n
            S.waitt("tensor", guard.get("ps2"))
            t_sqmm = {}
            for c in range(DC):
                sq = sq_ring[c % 2]
                if c >= 2:
                    S.wait("scalar", PE, t_sqmm[c - 2])
                t_sq = S.op("scalar", lambda e, c=c, sq=sq: e.activation(
                    sq[:], Xc(c, base, base + TB), AF.Square), SC)
                S.wait("tensor", SC, t_sq)
                t_sqmm[c] = S.op("tensor", lambda e, sq=sq, c=c: e.matmul(
                    pS2, ones_cb_sb[:], sq[:],
                    start=(c == 0), stop=(c == DC - 1)), PE)
            t_s2 = PE.n
            S.wait("vector", PE, t_s1)
            S.op("vector", lambda e: e.tensor_scalar(row(R_M), pS1, 1.0 / D, None, ALU.mult), VE)
            S.wait("vector", PE, t_s2)
            t_ps2r = S.op("vector", lambda e: e.tensor_scalar(row(R_V), pS2, 1.0 / D, None, ALU.mult), VE)
            guard["ps1"] = (VE, t_ps2r)
            guard["ps2"] = (VE, t_ps2r)
            S.drain("vector")
            S.op("vector", lambda e: e.tensor_tensor(row(R_A1), row(R_M), row(R_M), ALU.mult), VE)
            S.drain("vector")
            t_v = S.op("vector", lambda e: e.tensor_tensor(row(R_V), row(R_V), row(R_A1), ALU.subtract), VE)
            S.wait("scalar", VE, t_v)
            if want_rms:
                S.op("scalar", lambda e: e.activation(row(R_A1), row(R_V), AF.Ln,
                                                      scale=cr(C_1PE2), bias=cr(C_E1E2)), SC)
                S.drain("scalar")
                S.op("scalar", lambda e: e.activation(row(R_RSRG), row(R_A1), AF.Exp, scale=cr(C_NHALF)), SC)
                S.drain("scalar")
                t_ln2r = S.op("scalar", lambda e: e.activation(row(R_MRSRG), row(R_V), AF.Ln, bias=cr(C_E1)), SC)
                S.wait("vector", SC, t_ln2r)
                t_sub = S.op("vector", lambda e: e.tensor_tensor(row(R_A2), row(R_A1), row(R_MRSRG), ALU.subtract), VE)
                S.wait("scalar", VE, t_sub)
                t_invrg = S.op("scalar", lambda e: e.activation(row(R_A1), row(R_A2), AF.Exp, scale=cr(C_HALF)), SC)
                S.wait("vector", SC, t_invrg)
                S.op("vector", lambda e: e.tensor_tensor(row(R_MRSRG), row(R_M), row(R_RSRG), ALU.mult), VE)
            else:
                S.op("scalar", lambda e: e.activation(row(R_A1), row(R_V), AF.Ln, bias=cr(C_E1)), SC)
                S.drain("scalar")
                t_rows_s = S.op("scalar", lambda e: e.activation(row(R_RSRG), row(R_A1), AF.Exp, scale=cr(C_NHALF)), SC)
                S.wait("vector", SC, t_rows_s)
                S.op("vector", lambda e: e.tensor_tensor(row(R_MRSRG), row(R_M), row(R_RSRG), ALU.mult), VE)
            S.drain("vector")
            return VE.n

        def ffn_sublayer(idx):
            S = Sched()
            sbl = ExitStack()
            mk = lambda name, shape, dt: sbl.enter_context(nc.sbuf_tensor(f"{name}_{idx}", shape, dt))
            sq_ring = [mk(f"sq{i}", [128, TB], BF16) for i in range(2)]
            y32_ring = [mk(f"y32_{i}", [128, TB], F32) for i in range(2)]
            tmp2_ring = [mk(f"tmp2_{i}", [128, TB], F32) for i in range(2)]
            Lsb = mk("Lsb", [8, TB], F32)
            Ltok = mk("Ltok", [128, QT * 8], F32)
            Msk = mk("Msk", [128, QT * 8], F32)
            M1 = mk("M1", [128, QT * 8], F32)
            M2 = mk("M2", [128, QT * 8], F32)
            gmax1 = mk("gmax1", [128, QT], F32)
            gmax2 = mk("gmax2", [128, QT], F32)
            gw = mk("gw", [128, 2 * QT], F32)
            C_tok = mk("C_tok", [128, QT * 8], BF16)
            Cexp = mk("Cexp", [8, T], BF16)
            cbrow_t = mk("cbrow", [1, 2 * T], BF16)
            cbrow = [cbrow_t[0:1, 0:T], cbrow_t[0:1, T:2 * T]]
            cb_sb = [mk(f"cb{i}", [128, T], BF16) for i in range(2)]
            gu_ring = [mk(f"gu{i}", [128, 2048], BF16) for i in range(NGU)]
            dn_ring = [mk(f"dn{i}", [128, 8 * 128], BF16) for i in range(NDN)]
            hid_sb = [mk(f"hid{i}", [128, T], BF16) for i in range(max(GROUPS))]
            sg_ring = [mk(f"sg{i}", [128, TH], BF16) for i in range(2)]
            tmpd_ring = [mk(f"tmpd{i}", [128, SW], F32) for i in range(2)]

            ps = ExitStack()
            pS1 = psum(ps, f"pS1_{idx}", [1, TB], F32)[0:1, 0:TB]
            pS2 = psum(ps, f"pS2_{idx}", [1, TB], F32)[0:1, 0:TB]
            pLg = psum(ps, f"pLg_{idx}", [8, TB], F32)[0:8, 0:TB]
            pB1 = psum(ps, f"pB1_{idx}", [128, TB], F32)[:, 0:TB]
            pB2 = psum(ps, f"pB2_{idx}", [128, TB], F32)[:, 0:TB]
            pB3 = psum(ps, f"pB3_{idx}", [128, TB], F32)[:, 0:TB]
            pLt = psum(ps, f"pLt_{idx}", [128, 8], F32)[:, 0:8]
            pCt = psum(ps, f"pCt_{idx}", [8, 128], BF16)[0:8, 0:128]

            guard = {}
            t_y_last = 0
            t_y_nb = {}
            t_ltok = 0
            for nb in range(NB):
                base = nb * TB
                t_rows = norm_stats_rows(S, base, pS1, pS2, sq_ring, True, guard)
                S.wait("tensor", VE, t_rows)
                S.waitt("tensor", guard.get("pB"))
                t_b1 = S.op("tensor", lambda e: e.matmul(pB1, ones_r_sb[:], row(R_RSRG), start=True, stop=True), PE)
                t_b2 = S.op("tensor", lambda e: e.matmul(pB2, ones_r_sb[:], row(R_MRSRG), start=True, stop=True), PE)
                t_b3 = S.op("tensor", lambda e: e.matmul(pB3, halves_r_sb[:], row(R_A1), start=True, stop=True), PE)
                S.waitt("tensor", guard.get("pLg"))
                t_router = 0
                lo, hi = base, base + TB
                for cp in range(0, DC, 2):
                    cs = (cp, cp + 1)
                    for c in cs:
                        S.waitt("vector", guard.get(("y32", c % 2)))
                        S.waitt("vector", guard.get(("y32pe", c % 2)))
                    S.wait("vector", PE, t_b2)
                    S.drain("vector")
                    for c in cs:
                        S.op("vector", lambda e, c=c, lo=lo, hi=hi: e.tensor_tensor(
                            y32_ring[c % 2][:], Xc(c, lo, hi), pB1, ALU.mult), VE)
                    S.drain("vector")
                    for c in cs:
                        t_y32 = S.op("vector", lambda e, c=c: e.tensor_tensor(
                            y32_ring[c % 2][:], y32_ring[c % 2][:], pB2, ALU.subtract), VE)
                    S.wait("scalar", VE, t_y32)
                    for c in cs:
                        t_y_last = S.op("scalar", lambda e, c=c, lo=lo, hi=hi: e.activation(
                            Yc(c, lo, hi), y32_ring[c % 2][:], AF.Copy,
                            scale=nw_sb[:, c:c + 1]), SC)
                        guard[("y32", c % 2)] = (SC, t_y_last)
                    S.wait("tensor", VE, t_y32)
                    for c in cs:
                        t_router = S.op("tensor", lambda e, c=c: e.matmul(
                            pLg, rtr2_sb[:, c * E:(c + 1) * E], y32_ring[c % 2][:],
                            start=(c == 0), stop=(c == DC - 1)), PE)
                        guard[("y32pe", c % 2)] = (PE, t_router)
                    S.wait("vector", PE, t_b3)
                    S.drain("vector")
                    for c in cs:
                        S.op("vector", lambda e, c=c: e.tensor_tensor(
                            tmp2_ring[c % 2][:], y32_ring[c % 2][:], pB3, ALU.mult), VE)
                    S.drain("vector")
                    for c in cs:
                        S.op("vector", lambda e, c=c, lo=lo, hi=hi: e.tensor_tensor(
                            Xc(c, lo, hi), Xc(c, lo, hi), tmp2_ring[c % 2][:], ALU.add), VE)
                guard["pB"] = (VE, VE.n)
                t_y_nb[nb] = t_y_last
                S.wait("scalar", PE, t_router)
                t_lsb = S.op("scalar", lambda e: e.copy(Lsb[:], pLg), SC)
                guard["pLg"] = (SC, t_lsb)
                for j in range(TB // 128):
                    S.wait("tensor", SC, t_lsb)
                    if t_ltok:
                        S.wait("tensor", VE, t_ltok)
                    t_lt = S.op("tensor", lambda e, j=j: e.transpose(
                        pLt, Lsb[:, j * 128:(j + 1) * 128], idf_sb[:]), PE)
                    S.wait("vector", PE, t_lt)
                    q = nb * (TB // 128) + j
                    t_ltok = S.op("vector", lambda e, q=q: e.tensor_copy(
                        Ltok[:, q * 8:(q + 1) * 8], pLt), VE)

            L3 = Ltok[:].rearrange("p (g o) -> p g o", o=8)
            M3 = lambda t: t[:].rearrange("p (g o) -> p g o", o=8)
            g1b = gmax1[:].rearrange("p (g o) -> p g o", o=1).broadcast_to([128, QT, 8])
            g2b = gmax2[:].rearrange("p (g o) -> p g o", o=1).broadcast_to([128, QT, 8])
            S.drain("vector")
            S.op("vector", lambda e: e.reduce_max(
                gmax1[:].rearrange("p (g o) -> p g o", o=1), L3, axis=AX.X), VE)
            S.drain("vector")
            S.op("vector", lambda e: e.tensor_tensor(M3(M1), L3, g1b, ALU.is_ge), VE)
            S.drain("vector")
            S.op("vector", lambda e: e.scalar_tensor_tensor(
                Msk[:], M1[:], NEG_BIG, Ltok[:], ALU.mult, ALU.add), VE)
            S.drain("vector")
            S.op("vector", lambda e: e.reduce_max(
                gmax2[:].rearrange("p (g o) -> p g o", o=1), M3(Msk), axis=AX.X), VE)
            S.drain("vector")
            S.op("vector", lambda e: e.tensor_tensor(M3(M2), M3(Msk), g2b, ALU.is_ge), VE)
            t_d21 = S.op("vector", lambda e: e.tensor_tensor(
                gw[:, QT:2 * QT], gmax2[:], gmax1[:], ALU.subtract), VE)
            S.wait("scalar", VE, t_d21)
            t_w2 = S.op("scalar", lambda e: e.activation(
                gw[:, QT:2 * QT], gw[:, QT:2 * QT], AF.Sigmoid), SC)
            S.wait("vector", SC, t_w2)
            S.op("vector", lambda e: e.tensor_scalar(
                gw[:, 0:QT], gw[:, QT:2 * QT], -1.0, 1.0, ALU.mult, ALU.add), VE)
            S.drain("vector")
            w1b = gw[:, 0:QT].rearrange("p (g o) -> p g o", o=1).broadcast_to([128, QT, 8])
            w2b = gw[:, QT:2 * QT].rearrange("p (g o) -> p g o", o=1).broadcast_to([128, QT, 8])
            S.op("vector", lambda e: e.tensor_tensor(M3(M1), M3(M1), w1b, ALU.mult), VE)
            S.op("vector", lambda e: e.tensor_tensor(M3(M2), M3(M2), w2b, ALU.mult), VE)
            S.drain("vector")
            S.op("vector", lambda e: e.tensor_tensor(M1[:], M1[:], M2[:], ALU.add), VE)
            S.drain("vector")
            t_ctok = S.op("vector", lambda e: e.tensor_scalar(
                C_tok[:], M1[:], ha_sb[:], None, ALU.mult), VE)
            t_cexp = 0
            for q in range(QT):
                S.wait("tensor", VE, t_ctok)
                if t_cexp:
                    S.wait("tensor", VE, t_cexp)
                t_ct = S.op("tensor", lambda e, q=q: e.transpose(
                    pCt, C_tok[:, q * 8:(q + 1) * 8], idb_sb[:]), PE)
                S.wait("vector", PE, t_ct)
                t_cexp = S.op("vector", lambda e, q=q: e.tensor_copy(
                    Cexp[:, q * 128:(q + 1) * 128], pCt), VE)

            ps.close()
            ps2 = ExitStack()
            GU = [psum(ps2, f"pGU{i}_{idx}", [128, TH], F32)[:, 0:TH] for i in range(4)]
            gu_fr = {i: None for i in range(4)}
            slot_gu = {}
            slot_dn = {}
            cb_fr = {}
            cbrow_fr = {}
            t_cb_ready = {}
            t_bcast_prev = 0
            for e_i in range(E):
                sl = e_i % 2
                S.wait("sync", VE, t_cexp)
                S.wait("sync", PE, t_bcast_prev)
                t_cbrow = S.op("sync", lambda e, e_i=e_i, sl=sl: e.dma_start(
                    cbrow[sl], Cexp[e_i:e_i + 1, :]), DS)

                def cb_broadcast(sl=sl, t_cbrow=t_cbrow):
                    for j in range(T // SW):
                        S.wait("tensor", DS, t_cbrow)
                        S.waitt("tensor", gu_fr[0])
                        S.waitt("tensor", cb_fr.get(sl))
                        t_bc = S.op("tensor", lambda e, j=j, sl=sl: e.matmul(
                            GU[0][:, 0:SW], ones_rb_sb[:], cbrow[sl][:, j * SW:(j + 1) * SW],
                            start=True, stop=True), PE)
                        S.wait("scalar", PE, t_bc)
                        t_cbj = S.op("scalar", lambda e, j=j, sl=sl: e.copy(
                            cb_sb[sl][:, j * SW:(j + 1) * SW], GU[0][:, 0:SW]), SC)
                        gu_fr[0] = (SC, t_cbj)
                    cbrow_fr[sl] = (PE, PE.n)
                    t_cb_ready[sl] = SC.n

                if e_i == 0:
                    S.wait("tensor", SC, t_y_last)
                    S.wait("tensor", VE, t_cexp)

                cb_done = False
                k0 = 0
                for g_i, gsz in enumerate(GROUPS):
                    t_hid = {}
                    for i_l in range(gsz):
                        mm_i = k0 + i_l
                        slot = (e_i * NPAIR + mm_i) % NGU
                        S.waitt("gpsimd", slot_gu.get(slot))
                        t_w = S.op("gpsimd", lambda e, e_i=e_i, mm_i=mm_i, slot=slot: e.dma_start(
                            gu_ring[slot][:], gup.ap()[e_i, mm_i]), DWS[slot])
                        t_gate_nh = {}
                        t_up_nh = {}
                        S.wait("tensor", DWS[slot], t_w)
                        for nh in range(NH):
                            S.waitt("tensor", gu_fr[nh])
                            for ns in range(NS):
                                S.wait("tensor", SC, t_y_nb[(nh * TH + ns * SW) // TB])
                                for k in range(DC):
                                    S.op("tensor", lambda e, k=k, ns=ns, nh=nh, slot=slot: e.matmul(
                                        GU[nh][:, ns * SW:(ns + 1) * SW],
                                        gu_ring[slot][:, k * 256: k * 256 + 128],
                                        Yc(k, nh * TH + ns * SW, nh * TH + (ns + 1) * SW),
                                        start=(k == 0), stop=(k == DC - 1)),
                                        PE if (k == DC - 1 and ns == NS - 1) else None)
                            t_gate_nh[nh] = PE.n
                        for nh in range(NH):
                            S.waitt("tensor", gu_fr[2 + nh])
                            for ns in range(NS):
                                for k in range(DC):
                                    S.op("tensor", lambda e, k=k, ns=ns, nh=nh, slot=slot: e.matmul(
                                        GU[2 + nh][:, ns * SW:(ns + 1) * SW],
                                        gu_ring[slot][:, k * 256 + 128: k * 256 + 256],
                                        Yc(k, nh * TH + ns * SW, nh * TH + (ns + 1) * SW),
                                        start=(k == 0), stop=(k == DC - 1)),
                                        PE if (k == DC - 1 and ns == NS - 1) else None)
                            t_up_nh[nh] = PE.n
                        t_pair = PE.n
                        slot_gu[slot] = (PE, t_pair)
                        for nh in range(NH):
                            sg = sg_ring[nh]
                            S.wait("scalar", PE, t_gate_nh[nh])
                            S.waitt("scalar", guard.get(("sg", nh)))
                            t_sg = S.op("scalar", lambda e, nh=nh, sg=sg: e.activation(
                                sg[:], GU[nh], AF.Silu), SC)
                            gu_fr[nh] = (SC, t_sg)
                            S.wait("vector", SC, t_sg)
                            S.wait("vector", PE, t_up_nh[nh])
                            t_h = S.op("vector", lambda e, i_l=i_l, nh=nh, sg=sg: e.tensor_tensor(
                                hid_sb[i_l][:, nh * TH:(nh + 1) * TH], sg[:], GU[2 + nh], ALU.mult), VE)
                            t_hid[(i_l, nh)] = t_h
                            gu_fr[2 + nh] = (VE, t_h)
                            guard[("sg", nh)] = (VE, t_h)
                    if not cb_done:
                        cb_broadcast()
                        t_bcast_prev = PE.n
                        cb_done = True
                    for md in range(NMD):
                        slot = (e_i * len(GROUPS) * NMD + g_i * NMD + md) % NDN
                        S.waitt("gpsimd", slot_dn.get(slot))
                        t_wd = S.op("gpsimd", lambda e, e_i=e_i, md=md, k0=k0, gsz=gsz, slot=slot: e.dma_start(
                            dn_ring[slot][:, 0:gsz * 128],
                            dwn.ap()[e_i, md][:, k0 * 128:(k0 + gsz) * 128]), DWS[NGU + slot])
                        dpar = (md % 2) * 2
                        for nh in range(NH):
                            S.waitt("tensor", gu_fr[dpar + nh])
                        S.wait("tensor", DWS[NGU + slot], t_wd)
                        for k in range(gsz):
                            lastk = (k == gsz - 1)
                            S.wait("tensor", VE, t_hid[(k, NH - 1)])
                            for nh in range(NH):
                                for ns in range(NS):
                                    S.op("tensor", lambda e, k=k, ns=ns, nh=nh, slot=slot, dpar=dpar, gsz=gsz: e.matmul(
                                        GU[dpar + nh][:, ns * SW:(ns + 1) * SW],
                                        dn_ring[slot][:, k * 128:(k + 1) * 128],
                                        hid_sb[k][:, nh * TH + ns * SW: nh * TH + (ns + 1) * SW],
                                        start=(k == 0), stop=(k == gsz - 1)),
                                        PE if (lastk and nh == NH - 1 and ns == NS - 1) else None)
                        t_dmm = PE.n
                        slot_dn[slot] = (PE, t_dmm)
                        S.wait("vector", PE, t_dmm)
                        S.wait("vector", SC, t_cb_ready[sl])
                        t_td = 0
                        for nh in range(NH):
                            for ns in range(NS):
                                td = tmpd_ring[ns % 2]
                                S.drain("vector")
                                S.op("vector", lambda e, td=td, ns=ns, nh=nh, dpar=dpar, sl=sl: e.tensor_tensor(
                                    td[:], GU[dpar + nh][:, ns * SW:(ns + 1) * SW],
                                    cb_sb[sl][:, nh * TH + ns * SW: nh * TH + (ns + 1) * SW],
                                    ALU.mult), VE)
                                S.drain("vector")
                                t_td = S.op("vector", lambda e, td=td, ns=ns, nh=nh, md=md: e.tensor_tensor(
                                    Xc(md, nh * TH + ns * SW, nh * TH + (ns + 1) * SW),
                                    Xc(md, nh * TH + ns * SW, nh * TH + (ns + 1) * SW),
                                    td[:], ALU.add), VE)
                        gu_fr[dpar] = (VE, t_td)
                        gu_fr[dpar + 1] = (VE, t_td)
                        cb_fr[sl] = (VE, t_td)
                    k0 += gsz

            S.emit(nc, f"ffn{idx}")
            ps2.close()
            sbl.close()

        def attn_sublayer():
            S = Sched()
            sbl = ExitStack()
            mk = lambda name, shape, dt: sbl.enter_context(nc.sbuf_tensor(f"{name}_a", shape, dt))
            sq_ring = [mk(f"sq{i}", [128, TB], BF16) for i in range(2)]
            a32_ring = [mk("a32", [128, TB], F32)] * 2
            v_sb = mk("v_sb", [128, QT * 1024], BF16)
            v_fm = mk("v_fm", [128, T], BF16)
            fm = mk("fm", [128, H * T], BF16)
            q_sb = mk("q_sb", [128, T], BF16)
            k_sb = mk("k_sb", [128, T], BF16)
            p_sb = mk("p_sb", [128, T], BF16)
            pT_sb = [mk(f"pT{i}", [128, 128], BF16) for i in range(3)]
            o_tok = mk("o_tok", [128, 256], BF16)
            lqlk = mk("lqlk", [1, 2 * T], BF16)
            lqr = lqlk[0:1, 0:T]
            lkr = lqlk[0:1, T:2 * T]
            nmax = mk("nmax", [128, 2], F32)
            bmax = mk("bmax", [128, 4], F32)
            rsum = mk("rsum", [128, 4], F32)
            rcp = mk("rcp", [128, 2], F32)
            wcmb = mk("wcmb", [128, 4], F32)
            ctmp16 = mk("ctmp16", [128, 16], F32)
            ocmb = a32_ring[0]
            wqk_ring = [mk(f"wqk{i}", [128, 1024], BF16) for i in range(3)]
            wo_ring = wqk_ring

            psn = ExitStack()
            pS1 = psum(psn, "pS1_a", [1, TB], F32)[0:1, 0:TB]
            pS2 = psum(psn, "pS2_a", [1, TB], F32)[0:1, 0:TB]
            pB1 = psum(psn, "pB1_a", [128, TB], F32)[:, 0:TB]
            pB2 = psum(psn, "pB2_a", [128, TB], F32)[:, 0:TB]
            guard = {}
            for nb in range(NB):
                base = nb * TB
                t_rows = norm_stats_rows(S, base, pS1, pS2, sq_ring, False, guard)
                S.wait("tensor", VE, t_rows)
                S.waitt("tensor", guard.get("pB"))
                t_b1 = S.op("tensor", lambda e: e.matmul(pB1, ones_r_sb[:], row(R_RSRG), start=True, stop=True), PE)
                t_b2 = S.op("tensor", lambda e: e.matmul(pB2, ones_r_sb[:], row(R_MRSRG), start=True, stop=True), PE)
                for c in range(DC):
                    a32 = a32_ring[c % 2]
                    lo, hi = base, base + TB
                    S.wait("vector", PE, t_b2)
                    S.drain("vector")
                    S.op("vector", lambda e, c=c, a32=a32, lo=lo, hi=hi: e.tensor_tensor(
                        a32[:], Xc(c, lo, hi), pB1, ALU.mult), VE)
                    S.drain("vector")
                    S.op("vector", lambda e, c=c, a32=a32, lo=lo, hi=hi: e.tensor_tensor(
                        Yc(c, lo, hi), a32[:], pB2, ALU.subtract), VE)
                guard["pB"] = (VE, VE.n)
            t_ln2 = VE.n
            psn.close()

            ps = ExitStack()
            pS = psum(ps, "pS_a", [128, min(2048, T)], F32)
            pQK = psum(ps, "pQK_a", [128, 512], F32)
            pAV = psum(ps, "pAV_a", [128, 512], F32)[:, 0:512]
            pTr2 = [psum(ps, f"pTr{i}_a", [128, 128], BF16)[:, 0:128] for i in range(2)]

            pqk_fr = None
            wqk_fr = {}
            qk_buf_fr = {0: None, 1: None}
            ps_rd = {0: None, 1: None}
            t_lrow_fr = {}
            tr_fr = {}
            pav_fr = {}
            pt_slot_fr = {}
            nmax_fr = {}
            rsum_fr = {}
            otok_fr = {}
            w_fr = {}
            ct_fr = {}
            gslot = [0]
            prev_te = [0]
            t_fm_last = 0
            v_fm_fr = None
            for h in range(H):
                for sel in range(3):
                    S.waitt("gpsimd", wqk_fr.get(sel))
                    t_wq = S.op("gpsimd", lambda e, h=h, sel=sel: e.dma_start(
                        wqk_ring[sel][:], wqk.ap()[sel, h]), DWS[NGU + NDN + sel])
                    dst = [q_sb, k_sb, v_fm][sel]
                    for nh in range(NH):
                        for ns in range(NS):
                            lo = nh * TH + ns * SW
                            S.wait("tensor", DWS[NGU + NDN + sel], t_wq)
                            if h == 0:
                                S.wait("tensor", VE, t_ln2)
                            S.waitt("tensor", pqk_fr)
                            for k in range(DC):
                                S.op("tensor", lambda e, k=k, lo=lo, sel=sel: e.matmul(
                                    pQK[:, 0:SW],
                                    wqk_ring[sel][:, k * 128:(k + 1) * 128],
                                    Yc(k, lo, lo + SW),
                                    start=(k == 0), stop=(k == DC - 1)),
                                    PE if k == DC - 1 else None)
                            t_mm = PE.n
                            if sel == 0:
                                S.wait("scalar", PE, t_mm)
                                S.waitt("scalar", qk_buf_fr[0])
                                t_ev = S.op("scalar", lambda e, lo=lo: e.activation(
                                    q_sb[:, lo:lo + SW], pQK[:, 0:SW], AF.Copy,
                                    scale=cc(C_ISQH)), SC)
                                pqk_fr = (SC, t_ev)
                            else:
                                S.wait("vector", PE, t_mm)
                                S.waitt("vector", qk_buf_fr[1] if sel == 1 else v_fm_fr)
                                t_ev = S.op("vector", lambda e, lo=lo, dst=dst: e.tensor_copy(
                                    dst[:, lo:lo + SW], pQK[:, 0:SW]), VE)
                                pqk_fr = (VE, t_ev)
                    wqk_fr[sel] = (PE, PE.n)
                t_vfm_ev = VE.n
                for tt in range(QT):
                    ptr = pTr2[tt % 2]
                    S.wait("tensor", VE, t_vfm_ev)
                    S.waitt("tensor", tr_fr.get(tt % 2))
                    t_vt = S.op("tensor", lambda e, tt=tt, ptr=ptr: e.transpose(
                        ptr, v_fm[:, tt * 128:(tt + 1) * 128], idb_sb[:]), PE)
                    S.wait("vector", PE, t_vt)
                    t_ev = S.op("vector", lambda e, tt=tt, h=h, ptr=ptr: e.tensor_copy(
                        v_sb[:, tt * 1024 + h * 128: tt * 1024 + (h + 1) * 128], ptr), VE)
                    tr_fr[tt % 2] = (VE, t_ev)
                v_fm_fr = (PE, PE.n)
                t_v_done = VE.n
                t_q_ready = SC.n
                t_k_ready = VE.n
                nrow = T // SW
                t_lrow_s = 0
                t_lrow_v = 0
                for sel in range(2):
                    src = q_sb if sel == 0 else k_sb
                    for j in range(nrow):
                        S.wait("tensor", SC, t_q_ready)
                        S.wait("tensor", VE, t_k_ready)
                        S.waitt("tensor", ps_rd[0])
                        S.waitt("tensor", ps_rd[1])
                        S.waitt("tensor", t_lrow_fr.get((sel ^ 1, j)))
                        S.waitt("tensor", t_lrow_fr.get((sel, j)))
                        t_r = S.op("tensor", lambda e, j=j, src=src, h=h: e.matmul(
                            pS[0:1, j * SW:(j + 1) * SW], dirs_sb[:, h:h + 1],
                            src[:, j * SW:(j + 1) * SW], start=True, stop=True), PE)
                        if sel == 0:
                            S.wait("scalar", PE, t_r)
                            t_lrow_s = S.op("scalar", lambda e, j=j, h=h: e.activation(
                                lqr[:, j * SW:(j + 1) * SW], pS[0:1, j * SW:(j + 1) * SW],
                                AF.Copy, scale=hs2_sb[0:1, h:h + 1]), SC)
                            t_lrow_fr[(sel, j)] = (SC, t_lrow_s)
                        else:
                            S.wait("vector", PE, t_r)
                            t_lrow_v = S.op("vector", lambda e, j=j: e.tensor_copy(
                                lkr[:, j * SW:(j + 1) * SW], pS[0:1, j * SW:(j + 1) * SW]), VE)
                            t_lrow_fr[(sel, j)] = (VE, t_lrow_v)
                subs = []
                sub_groups = {}
                for t_i in range(QT):
                    KT = (t_i + 1) * 128
                    cuts = [(0, KT)] if KT <= TH else [(0, TH), (TH, KT)]
                    for si, (lo, hi) in enumerate(cuts):
                        sub_groups.setdefault(t_i, []).append(len(subs))
                        subs.append((t_i, si, lo, hi))
                nsub = len(subs)
                geo_t = {}
                expb = {}
                avst = {}
                pending_o = []

                def emit_scores(sg):
                    t_i, si, lo, hi = subs[sg]
                    par = sg % 2
                    ws = hi - lo
                    nb = (ws + SW - 1) // SW
                    S.wait("tensor", SC, t_lrow_s)
                    S.wait("tensor", VE, max(t_lrow_v, t_v_done))
                    S.waitt("tensor", ps_rd.get(par))
                    tg = {}
                    for b in range(nb):
                        w = min(SW, ws - b * SW)
                        off = par * TH + b * SW
                        S.op("tensor", lambda e, t_i=t_i, lo=lo, b=b, w=w, off=off: e.matmul(
                            pS[:, off: off + w],
                            q_sb[:, t_i * 128:(t_i + 1) * 128],
                            k_sb[:, lo + b * SW: lo + b * SW + w],
                            start=True, stop=False), None)
                    for b in range(nb):
                        w = min(SW, ws - b * SW)
                        off = par * TH + b * SW
                        tg[b] = S.op("tensor", lambda e, t_i=t_i, lo=lo, b=b, w=w, off=off: e.matmul(
                            pS[:, off: off + w],
                            lqr[:, t_i * 128:(t_i + 1) * 128],
                            lkr[:, lo + b * SW: lo + b * SW + w],
                            start=False, stop=True), PE)
                    geo_t[sg] = tg

                def emit_softmax(sg, te_prev):
                    t_i, si, lo, hi = subs[sg]
                    par = sg % 2
                    ws = hi - lo
                    nb = (ws + SW - 1) // SW
                    last = (hi == (t_i + 1) * 128)
                    for b in range(nb):
                        w = min(SW, ws - b * SW)
                        off = par * TH + b * SW
                        S.wait("vector", PE, geo_t[sg][b])
                        if last and b == nb - 1:
                            doff = par * TH + (t_i * 128 - lo)
                            S.drain("vector")
                            S.op("vector", lambda e, doff=doff: e.tensor_tensor(
                                pS[:, doff:doff + 128],
                                pS[:, doff:doff + 128], cmask_sb[:], ALU.add), VE)
                            S.drain("vector")
                        S.op("vector", lambda e, b=b, w=w, off=off, par=par: e.reduce_max(
                            bmax[:, par * 2 + b: par * 2 + b + 1],
                            pS[:, off: off + w], axis=AX.X), VE)
                        S.drain("vector")
                    S.waitt("vector", nmax_fr.get(par))
                    t_m = S.op("vector", lambda e, par=par, nb=nb: e.reduce_max(
                        nmax[:, par:par + 1], bmax[:, par * 2: par * 2 + nb],
                        axis=AX.X, negate=True), VE)
                    S.wait("scalar", VE, t_m)
                    S.wait("scalar", PE, te_prev)
                    S.waitt("scalar", rsum_fr.get(par))
                    eb = {}
                    for b in range(nb):
                        w = min(SW, ws - b * SW)
                        off = par * TH + b * SW
                        if b:
                            S.drain("scalar")
                        eb[b] = S.op("scalar", lambda e, lo=lo, b=b, w=w, off=off, par=par: e.activation(
                            p_sb[:, lo + b * SW: lo + b * SW + w],
                            pS[:, off: off + w], AF.Exp,
                            bias=nmax[:, par:par + 1],
                            accum_out=rsum[0:128, par * 2 + b: par * 2 + b + 1]), SC)
                    expb[sg] = eb
                    ps_rd[par] = (SC, eb[nb - 1])
                    nmax_fr[par] = (SC, eb[nb - 1])

                def emit_trans_av(sg):
                    t_i, si, lo, hi = subs[sg]
                    ws = hi - lo
                    ntp = ws // 128
                    r = (t_i % 2) * 2 + si
                    pt_ev = {}
                    pt_slot = {}

                    def do_av(j, h=h):
                        S.waitt("tensor", pt_ev[j])
                        if j == 0:
                            S.waitt("tensor", pav_fr.get(r))
                        kb = lo // 128 + j
                        S.op("tensor", lambda e, j=j, kb=kb, r=r, sl=pt_slot[j], ntp=ntp, h=h: e.matmul(
                            pAV[:, r * 128:(r + 1) * 128], pT_sb[sl][:],
                            v_sb[:, kb * 1024 + h * 128: kb * 1024 + (h + 1) * 128],
                            start=(j == 0), stop=(j == ntp - 1)),
                            PE if j == ntp - 1 else None)
                        pt_slot_fr[pt_slot[j]] = (PE, PE.n)

                    for j in range(ntp):
                        slot = gslot[0] % 3
                        trp = gslot[0] % 2
                        gslot[0] += 1
                        pt_slot[j] = slot
                        ptr = pTr2[trp]
                        S.wait("tensor", SC, expb[sg][(j * 128) // SW])
                        S.waitt("tensor", tr_fr.get(trp))
                        t_trm = S.op("tensor", lambda e, lo=lo, j=j, ptr=ptr: e.transpose(
                            ptr, p_sb[:, lo + j * 128: lo + (j + 1) * 128], idb_sb[:]), PE)
                        S.wait("vector", PE, t_trm)
                        S.waitt("vector", pt_slot_fr.get(slot))
                        t_ev = S.op("vector", lambda e, sl=slot, ptr=ptr: e.tensor_copy(
                            pT_sb[sl][:], ptr), VE)
                        tr_fr[trp] = (VE, t_ev)
                        pt_ev[j] = (VE, t_ev)
                        if j >= 2:
                            do_av(j - 2)
                    for j in range(max(0, ntp - 2), ntp):
                        do_av(j)
                    avst[sg] = PE.n
                    return PE.n

                def emit_combine(t_i):
                    sgs = sub_groups[t_i]
                    tp = t_i % 2
                    eb_last = max(expb[sg][len(expb[sg]) - 1] for sg in sgs)
                    S.wait("vector", SC, eb_last)
                    sv = []
                    for idx, sg in enumerate(sgs):
                        par = sg % 2
                        nb = len(expb[sg])
                        if nb == 2:
                            S.drain("vector")
                            S.op("vector", lambda e, par=par, idx=idx, tp=tp: e.tensor_tensor(
                                ctmp16[:, tp * 8 + 1 + idx: tp * 8 + 2 + idx],
                                rsum[:, par * 2: par * 2 + 1],
                                rsum[:, par * 2 + 1: par * 2 + 2], ALU.add), VE)
                            sv.append(ctmp16[:, tp * 8 + 1 + idx: tp * 8 + 2 + idx])
                        else:
                            sv.append(rsum[:, par * 2: par * 2 + 1])
                    if len(sgs) == 1:
                        par = sgs[0] % 2
                        S.drain("vector")
                        S.op("vector", lambda e, tp=tp, s0=sv[0]: e.reciprocal(
                            rcp[:, tp:tp + 1], s0), VE)
                        S.drain("vector")
                        S.wait("vector", PE, avst[sgs[0]])
                        S.waitt("vector", otok_fr.get(tp))
                        r0 = tp * 2
                        t_ot = S.op("vector", lambda e, tp=tp, r0=r0: e.tensor_scalar(
                            o_tok[:, tp * 128:(tp + 1) * 128],
                            pAV[:, r0 * 128:(r0 + 1) * 128],
                            rcp[:, tp:tp + 1], None, ALU.mult), VE)
                        pav_fr[r0] = (VE, t_ot)
                        rsum_fr[par] = (VE, t_ot)
                    else:
                        sp0, sp1 = sgs[0] % 2, sgs[1] % 2
                        S.drain("vector")
                        S.op("vector", lambda e, tp=tp: e.tensor_tensor(
                            ctmp16[:, tp * 8: tp * 8 + 1],
                            nmax[:, 0:1], nmax[:, 1:2], ALU.min), VE)
                        S.drain("vector")
                        S.waitt("vector", ct_fr.get(tp))
                        S.op("vector", lambda e, tp=tp: e.tensor_scalar(
                            ctmp16[:, tp * 8 + 6: tp * 8 + 8], nmax[:, 0:2],
                            ctmp16[:, tp * 8: tp * 8 + 1], None, ALU.subtract), VE)
                        S.drain("vector")
                        t_dd = S.op("vector", lambda e, tp=tp: e.tensor_scalar(
                            ctmp16[:, tp * 8 + 6: tp * 8 + 8],
                            ctmp16[:, tp * 8 + 6: tp * 8 + 8],
                            -1.0, None, ALU.mult), VE)
                        S.wait("scalar", VE, t_dd)
                        S.waitt("scalar", w_fr.get(tp))
                        t_w = S.op("scalar", lambda e, tp=tp: e.activation(
                            wcmb[:, tp * 2: tp * 2 + 2],
                            ctmp16[:, tp * 8 + 6: tp * 8 + 8], AF.Exp), SC)
                        ct_fr[tp] = (SC, t_w)
                        S.wait("vector", SC, t_w)
                        S.drain("vector")
                        S.op("vector", lambda e, tp=tp, sp0=sp0, s0=sv[0]: e.tensor_tensor(
                            ctmp16[:, tp * 8 + 3: tp * 8 + 4], s0,
                            wcmb[:, tp * 2 + sp0: tp * 2 + sp0 + 1], ALU.mult), VE)
                        S.op("vector", lambda e, tp=tp, sp1=sp1, s1=sv[1]: e.tensor_tensor(
                            ctmp16[:, tp * 8 + 4: tp * 8 + 5], s1,
                            wcmb[:, tp * 2 + sp1: tp * 2 + sp1 + 1], ALU.mult), VE)
                        S.drain("vector")
                        S.op("vector", lambda e, tp=tp: e.tensor_tensor(
                            ctmp16[:, tp * 8 + 5: tp * 8 + 6],
                            ctmp16[:, tp * 8 + 3: tp * 8 + 4],
                            ctmp16[:, tp * 8 + 4: tp * 8 + 5], ALU.add), VE)
                        S.drain("vector")
                        S.op("vector", lambda e, tp=tp: e.reciprocal(
                            rcp[:, tp:tp + 1], ctmp16[:, tp * 8 + 5: tp * 8 + 6]), VE)
                        S.drain("vector")
                        S.op("vector", lambda e, tp=tp, sp0=sp0: e.tensor_tensor(
                            ctmp16[:, tp * 8 + 3: tp * 8 + 4],
                            wcmb[:, tp * 2 + sp0: tp * 2 + sp0 + 1],
                            rcp[:, tp:tp + 1], ALU.mult), VE)
                        t_c1 = S.op("vector", lambda e, tp=tp, sp1=sp1: e.tensor_tensor(
                            ctmp16[:, tp * 8 + 4: tp * 8 + 5],
                            wcmb[:, tp * 2 + sp1: tp * 2 + sp1 + 1],
                            rcp[:, tp:tp + 1], ALU.mult), VE)
                        w_fr[tp] = (VE, t_c1)
                        S.drain("vector")
                        S.wait("vector", PE, avst[sgs[1]])
                        r0, r1 = tp * 2, tp * 2 + 1
                        S.op("vector", lambda e, tp=tp, r0=r0: e.tensor_scalar(
                            ocmb[:, 0:128], pAV[:, r0 * 128:(r0 + 1) * 128],
                            ctmp16[:, tp * 8 + 3: tp * 8 + 4], None, ALU.mult), VE)
                        S.op("vector", lambda e, tp=tp, r1=r1: e.tensor_scalar(
                            ocmb[:, 128:256], pAV[:, r1 * 128:(r1 + 1) * 128],
                            ctmp16[:, tp * 8 + 4: tp * 8 + 5], None, ALU.mult), VE)
                        S.drain("vector")
                        S.waitt("vector", otok_fr.get(tp))
                        t_ot = S.op("vector", lambda e, tp=tp: e.tensor_tensor(
                            o_tok[:, tp * 128:(tp + 1) * 128],
                            ocmb[:, 0:128], ocmb[:, 128:256], ALU.add), VE)
                        pav_fr[r0] = (VE, t_ot)
                        pav_fr[r1] = (VE, t_ot)
                        rsum_fr[sp0] = (VE, t_ot)
                        rsum_fr[sp1] = (VE, t_ot)
                    pending_o.append((t_i, VE.n))

                def emit_o(t_done, t_ready):
                    tp = t_done % 2
                    trp = gslot[0] % 2
                    gslot[0] += 1
                    ptro = pTr2[trp]
                    S.wait("tensor", VE, t_ready)
                    S.waitt("tensor", tr_fr.get(trp))
                    t_otr = S.op("tensor", lambda e, tp=tp, ptro=ptro: e.transpose(
                        ptro, o_tok[:, tp * 128:(tp + 1) * 128], idb_sb[:]), PE)
                    otok_fr[tp] = (PE, t_otr)
                    S.wait("vector", PE, t_otr)
                    t_fme = S.op("vector", lambda e, h=h, t_done=t_done, ptro=ptro: e.tensor_copy(
                        fm[:, h * T + t_done * 128: h * T + (t_done + 1) * 128], ptro), VE)
                    tr_fr[trp] = (VE, t_fme)
                    return t_fme

                for s in range(nsub):
                    emit_scores(s)
                    if pending_o:
                        t_fm_last = emit_o(*pending_o.pop(0))
                    te = emit_trans_av(s - 1) if s > 0 else prev_te[0]
                    if s > 0:
                        pt_i, psi, plo, phi = subs[s - 1]
                        if phi == (pt_i + 1) * 128:
                            emit_combine(pt_i)
                    emit_softmax(s, te)
                emit_trans_av(nsub - 1)
                emit_combine(subs[nsub - 1][0])
                while pending_o:
                    t_fm_last = emit_o(*pending_o.pop(0))
                prev_te[0] = PE.n
                qk_buf_fr[0] = (PE, PE.n)
                qk_buf_fr[1] = (PE, PE.n)

            wo_fr = {0: wqk_fr.get(0), 1: wqk_fr.get(1)}
            op_fr = {0: ps_rd[0], 1: ps_rd[1]}
            for md in range(NMD):
                slot = md % 2
                S.waitt("gpsimd", wo_fr.get(slot))
                t_wo = S.op("gpsimd", lambda e, md=md, slot=slot: e.dma_start(
                    wo_ring[slot][:], wo.ap()[md]), DWS[NGU + NDN + slot])
                for nh in range(NH):
                    opar = (md * NH + nh) % 2
                    if 2 * TH <= min(2048, T):
                        dst = pS[:, opar * TH:(opar + 1) * TH]
                    else:
                        dst = pS[:, 0:TH]
                        opar = 0
                    S.wait("tensor", DWS[NGU + NDN + slot], t_wo)
                    S.wait("tensor", VE, t_fm_last)
                    S.waitt("tensor", op_fr[opar])
                    for k in range(DC):
                        last = (k == DC - 1)
                        for ns in range(NS):
                            S.op("tensor", lambda e, k=k, ns=ns, nh=nh, slot=slot, dst=dst: e.matmul(
                                dst[:, ns * SW:(ns + 1) * SW],
                                wo_ring[slot][:, k * 128:(k + 1) * 128],
                                fm[:, k * T + nh * TH + ns * SW: k * T + nh * TH + (ns + 1) * SW],
                                start=(k == 0), stop=(k == DC - 1)),
                                PE if (last and ns == NS - 1) else None)
                    t_omm = PE.n
                    S.wait("vector", PE, t_omm)
                    t_oadd = S.op("vector", lambda e, md=md, nh=nh, dst=dst: e.tensor_tensor(
                        Xc(md, nh * TH, (nh + 1) * TH),
                        Xc(md, nh * TH, (nh + 1) * TH), dst[:, 0:TH], ALU.add), VE)
                    op_fr[opar] = (VE, t_oadd)
                wo_fr[slot] = (PE, PE.n)

            S.emit(nc, "attn")
            ps.close()
            sbl.close()

        if 1 in parts:
            ffn_sublayer(1)
        if 2 in parts:
            attn_sublayer()
        if 3 in parts:
            ffn_sublayer(2)

        S = Sched()
        S.op("sync", lambda e: e.dma_start(
            yout.ap().rearrange("(c p) t -> p c t", p=128),
            Xs[:].rearrange("p (c t) -> p c t", c=DC)), DS)
        S.wait("sync", DS, DS.n)
        S.emit(nc, "out")

    return nc


def prepare_in_maps(inputs, T, n_cores):
    x = np.asarray(inputs["L0_attn_x"], dtype=np.float32)
    w_qkv = np.ascontiguousarray(np.asarray(inputs["L0_attn_w_qkv"], dtype=np.float32))
    w_out = np.asarray(inputs["L0_attn_w_out"], dtype=np.float32)
    head_scales = np.asarray(inputs["L0_attn_head_scales"], dtype=np.float32)
    nw = np.asarray(inputs["L0_ffn_norm_weight"], dtype=np.float32)
    rw = np.asarray(inputs["L0_ffn_router_weight"], dtype=np.float32)
    guw = np.asarray(inputs["L0_ffn_gate_up_weight"], dtype=np.float32)
    dww = np.asarray(inputs["L0_ffn_down_weight"], dtype=np.float32)
    alpha = np.asarray(inputs["L0_ffn_residual_alpha"], dtype=np.float32).reshape(1, 1)

    gup_s = np.empty((E, NPAIR, 128, 2048), np.float32)
    for e in range(E):
        WT = guw[e].T
        G = WT[:, :HID].reshape(DC, 128, NPAIR, 128)
        U = WT[:, HID:].reshape(DC, 128, NPAIR, 128)
        pair = np.concatenate([G, U], axis=3)
        gup_s[e] = pair.transpose(2, 1, 0, 3).reshape(NPAIR, 128, 2048)
    dwn_s = np.empty((E, NMD, 128, HID), np.float32)
    for e in range(E):
        DT = dww[e].T
        A = DT.reshape(NPAIR, 128, NMD, 128)
        dwn_s[e] = A.transpose(2, 1, 0, 3).reshape(NMD, 128, HID)
    Q = w_qkv.reshape(DC, 128, 3, H, 128)
    wqk_s = np.ascontiguousarray(Q.transpose(2, 3, 1, 0, 4).reshape(3, H, 128, 1024))
    O = w_out.reshape(DC, 128, NMD, 128)
    wo_s = np.ascontiguousarray(O.transpose(2, 1, 0, 3).reshape(NMD, 128, 1024))
    rtr_s = np.ascontiguousarray(rw.T.reshape(DC, 128, E).transpose(1, 0, 2).reshape(128, DC * E))
    nw_s = np.ascontiguousarray(nw.reshape(DC, 128).T)
    hs_s = head_scales.reshape(1, H)
    t = np.array([[(i >> 2 & 1) * 2 - 1, (i >> 1 & 1) * 2 - 1, (i & 1) * 2 - 1]
                  for i in range(8)], dtype=np.float32) / np.sqrt(3.0, dtype=np.float32)
    dirs = np.zeros((128, H), np.float32)
    dirs[0:3, :] = t.T
    shared = dict(
        gup=gup_s, dwn=dwn_s, wqk=wqk_s, wo=wo_s, rtr=rtr_s,
        nwc=nw_s, hsr=hs_s, alr=alpha,
        dirs_t=dirs.astype(ml_dtypes.bfloat16),
        id_bf=np.eye(128, dtype=ml_dtypes.bfloat16),
        id_f32=np.eye(8, dtype=np.float32),
        ones_r=np.ones((1, 128), np.float32),
        ones_rb=np.ones((1, 128), ml_dtypes.bfloat16),
        ones_c=np.ones((128, 1), np.float32),
        ones_cb=np.ones((128, 1), ml_dtypes.bfloat16),
        cmask=np.where(np.arange(128)[:, None] >= np.arange(128)[None, :],
                       0.0, NEG_BIG).astype(ml_dtypes.bfloat16),
        cst=np.tile(np.array([[1.0 + EPS_RMS, EPS_LN * EPS_RMS, -0.5, EPS_LN,
                               1.0 / math.sqrt(HD), 0.5, 0.0, 0.0]], np.float32),
                    (128, 1)),
        halves_r=np.full((1, 128), 0.5, np.float32),
    )
    in_maps = []
    for b in range(n_cores):
        m = dict(shared)
        m["xT"] = np.ascontiguousarray(x[b, :T, :].T)
        in_maps.append(m)
    return in_maps


def kernel(**inputs):
    T = 2048
    n_cores = 8
    in_maps = prepare_in_maps(inputs, T, n_cores)
    nc = build_program(T)
    res = run_bass_kernel_spmd(nc, in_maps, core_ids=list(range(n_cores)))
    out = np.empty((n_cores, T, D), np.float32)
    for b in range(n_cores):
        out[b] = res.results[b]["yout"].T
    return out


# version 12 (final)
# speedup vs baseline: 1.1312x; 1.0624x over previous
import math
from contextlib import ExitStack

import numpy as np
import ml_dtypes
import concourse.bass as bass
import concourse.mybir as mybir
from concourse.bass_utils import run_bass_kernel_spmd

F32 = mybir.dt.float32
BF16 = mybir.dt.bfloat16
AF = mybir.ActivationFunctionType
ALU = mybir.AluOpType
AX = mybir.AxisListType

D = 1024
DC = D // 128
HID = 2816
E = 8
H = 8
HD = 128
NPAIR = 22
GROUPS = (8, 8, 6)
NMD = 8
EPS_LN = 1e-5
EPS_RMS = 1e-6
NEG_BIG = -1.0e30
NGU = 3
NDN = 2


class Ctr:
    def __init__(self, sem, step=1):
        self.sem = sem
        self.n = 0
        self.step = step


class Sched:
    ENGINES = ("tensor", "scalar", "vector", "gpsimd", "sync")

    def __init__(self):
        self.ops = {e: [] for e in self.ENGINES}
        self.last_wait = {e: {} for e in self.ENGINES}

    def wait(self, eng, ctr, val):
        if val is None or val <= 0:
            return
        lw = self.last_wait[eng]
        k = id(ctr)
        if lw.get(k, 0) >= val:
            return
        lw[k] = val
        self.ops[eng].append(("w", ctr.sem, val))

    def waitt(self, eng, ticket):
        if ticket is not None:
            self.wait(eng, ticket[0], ticket[1])

    def op(self, eng, fn, inc=None):
        self.ops[eng].append(("o", fn, inc))
        if inc is not None:
            inc.n += inc.step
            return inc.n

    def drain(self, eng):
        self.ops[eng].append(("d", None, None))

    def emit(self, nc, name):
        with nc.Block(name) as blk:
            for ename in self.ENGINES:
                ops = self.ops[ename]
                if not ops:
                    continue

                def body(e, ops=ops):
                    for kind, a, b in ops:
                        if kind == "w":
                            e.wait_ge(a, b)
                        elif kind == "d":
                            e.drain()
                        else:
                            inst = a(e)
                            if b is not None:
                                inst.then_inc(b.sem, b.step)
                getattr(blk, ename)(body)


def build_program(T, parts=(1, 2, 3)):
    assert T % 128 == 0 and 128 <= T <= 2048
    QT = T // 128
    TB = min(512, T)
    NB = T // TB
    TH = min(1024, T)
    NH = T // TH
    SW = min(512, TH)
    NS = TH // SW

    nc = bass.Bass("TRN2", target_bir_lowering=False)

    xT = nc.dram_tensor("xT", [D, T], F32, kind="ExternalInput")
    gup = nc.dram_tensor("gup", [E, NPAIR, 128, 2048], F32, kind="ExternalInput")
    dwn = nc.dram_tensor("dwn", [E, NMD, 128, HID], F32, kind="ExternalInput")
    wqk = nc.dram_tensor("wqk", [3, H, 128, 1024], F32, kind="ExternalInput")
    wo = nc.dram_tensor("wo", [NMD, 128, 1024], F32, kind="ExternalInput")
    rtr = nc.dram_tensor("rtr", [128, DC * E], F32, kind="ExternalInput")
    nwc = nc.dram_tensor("nwc", [128, DC], F32, kind="ExternalInput")
    hsr = nc.dram_tensor("hsr", [1, H], F32, kind="ExternalInput")
    alr = nc.dram_tensor("alr", [1, 1], F32, kind="ExternalInput")
    dirs_t = nc.dram_tensor("dirs_t", [128, H], BF16, kind="ExternalInput")
    id_bf = nc.dram_tensor("id_bf", [128, 128], BF16, kind="ExternalInput")
    id_f32 = nc.dram_tensor("id_f32", [8, 8], F32, kind="ExternalInput")
    ones_r = nc.dram_tensor("ones_r", [1, 128], F32, kind="ExternalInput")
    halves_r = nc.dram_tensor("halves_r", [1, 128], F32, kind="ExternalInput")
    ones_rb = nc.dram_tensor("ones_rb", [1, 128], BF16, kind="ExternalInput")
    ones_c = nc.dram_tensor("ones_c", [128, 1], F32, kind="ExternalInput")
    ones_cb = nc.dram_tensor("ones_cb", [128, 1], BF16, kind="ExternalInput")
    cmask = nc.dram_tensor("cmask", [128, 128], BF16, kind="ExternalInput")
    cst = nc.dram_tensor("cst", [128, 8], F32, kind="ExternalInput")
    yout = nc.dram_tensor("yout", [D, T], F32, kind="ExternalOutput")

    es = ExitStack()
    with es:
        s_pe = es.enter_context(nc.semaphore("s_pe"))
        s_act = es.enter_context(nc.semaphore("s_act"))
        s_dve = es.enter_context(nc.semaphore("s_dve"))
        s_ds = es.enter_context(nc.semaphore("s_ds"))
        s_dma = [es.enter_context(nc.semaphore(f"s_dma{i}")) for i in range(10)]
        PE, SC, VE = Ctr(s_pe), Ctr(s_act), Ctr(s_dve)
        DS = Ctr(s_ds, 16)
        DWS = [Ctr(s, 16) for s in s_dma]

        sb = lambda name, shape, dt: es.enter_context(nc.sbuf_tensor(name, shape, dt))
        Xs = sb("X", [128, DC * T], F32)
        Ys = sb("Y", [128, DC * T], BF16)
        nw_sb = sb("nw_sb", [128, DC], F32)
        rtr_sb = sb("rtr_sb", [128, DC * E], F32)
        rtr2_sb = rtr_sb
        hs_sb = sb("hs_sb", [1, H], F32)
        hs2_sb = sb("hs2_sb", [1, H], F32)
        al_sb = sb("al_sb", [1, 1], F32)
        ha_sb = sb("ha_sb", [128, 1], F32)
        dirs_sb = sb("dirs_sb", [128, H], BF16)
        idb_sb = sb("idb_sb", [128, 128], BF16)
        idf_sb = sb("idf_sb", [8, 8], F32)
        ones_r_sb = sb("ones_r_sb", [1, 128], F32)
        halves_r_sb = sb("halves_r_sb", [1, 128], F32)
        ones_rb_sb = sb("ones_rb_sb", [1, 128], BF16)
        ones_c_sb = sb("ones_c_sb", [128, 1], F32)
        ones_cb_sb = sb("ones_cb_sb", [128, 1], BF16)
        cmask_sb = sb("cmask_sb", [128, 128], BF16)
        cst_sb = sb("cst_sb", [128, 8], F32)
        rows = sb("rows", [1, 5 * TB], F32)

        def Xc(c, lo=0, hi=None):
            hi = T if hi is None else hi
            return Xs[:, c * T + lo: c * T + hi]

        def Yc(c, lo=0, hi=None):
            hi = T if hi is None else hi
            return Ys[:, c * T + lo: c * T + hi]

        R_M, R_V, R_RSRG, R_MRSRG, R_A1 = range(5)
        R_A2 = R_V
        C_1PE2, C_E1E2, C_NHALF, C_E1, C_ISQH, C_HALF = range(6)

        def cc(i):
            return cst_sb[:, i:i + 1]

        def cr(i):
            return cst_sb[0:1, i:i + 1]

        def row(i, w=TB):
            return rows[0:1, i * TB: i * TB + w]

        def bank_pad(n, dt):
            q = 512 if dt == F32 else 1024
            return ((n + q - 1) // q) * q

        def psum(stack, name, shape, dt):
            padded = [shape[0], bank_pad(shape[1], dt)]
            t = stack.enter_context(nc.psum_tensor(name, padded, dt))
            return t

        S = Sched()
        loads = [
            (Xs[:].rearrange("p (c t) -> p c t", c=DC),
             xT.ap().rearrange("(c p) t -> p c t", p=128)),
            (nw_sb[:], nwc.ap()), (rtr_sb[:], rtr.ap()), (hs_sb[:], hsr.ap()),
            (al_sb[:], alr.ap()), (dirs_sb[:], dirs_t.ap()), (idb_sb[:], id_bf.ap()),
            (idf_sb[:], id_f32.ap()), (ones_r_sb[:], ones_r.ap()),
            (halves_r_sb[:], halves_r.ap()),
            (ones_rb_sb[:], ones_rb.ap()), (ones_c_sb[:], ones_c.ap()),
            (ones_cb_sb[:], ones_cb.ap()), (cmask_sb[:], cmask.ap()),
            (cst_sb[:], cst.ap()),
        ]
        for dst, src in loads:
            S.op("sync", lambda e, d=dst, s=src: e.dma_start(d, s), DS)
        S.wait("vector", DS, DS.n)
        S.op("vector", lambda e: e.tensor_scalar(hs2_sb[:], hs_sb[:], math.sqrt(HD), None, ALU.mult), VE)
        S.op("vector", lambda e: e.tensor_scalar(al_sb[:], al_sb[:], -0.5, 2.0, ALU.max, ALU.min), VE)
        S.drain("vector")
        S.op("vector", lambda e: e.tensor_scalar(al_sb[:], al_sb[:], 0.5, None, ALU.mult), VE)
        S.drain("vector")
        for c in range(DC):
            S.op("vector", lambda e, c=c: e.tensor_scalar(
                rtr2_sb[:, c * E:(c + 1) * E], rtr_sb[:, c * E:(c + 1) * E],
                nw_sb[:, c:c + 1], None, ALU.mult), VE)
        t_pre_v = VE.n
        with nc.psum_tensor("p_ha", [128, 512], F32) as p_ha_full:
            p_ha = p_ha_full[:, 0:1]
            S.wait("tensor", VE, t_pre_v)
            t_ha = S.op("tensor", lambda e: e.matmul(p_ha, ones_r_sb[:], al_sb[:],
                                                     start=True, stop=True), PE)
            S.wait("vector", PE, t_ha)
            S.op("vector", lambda e: e.tensor_copy(ha_sb[:], p_ha), VE)
            S.emit(nc, "prologue")

        def norm_stats_rows(S, base, pS1, pS2, sq_ring, want_rms, guard):
            S.waitt("tensor", guard.get("ps1"))
            for c in range(DC):
                S.op("tensor", lambda e, c=c: e.matmul(
                    pS1, ones_c_sb[:], Xc(c, base, base + TB),
                    start=(c == 0), stop=(c == DC - 1)), PE if c == DC - 1 else None)
            t_s1 = PE.n
            S.waitt("tensor", guard.get("ps2"))
            t_sqmm = {}
            for c in range(DC):
                sq = sq_ring[c % 2]
                if c >= 2:
                    S.wait("scalar", PE, t_sqmm[c - 2])
                t_sq = S.op("scalar", lambda e, c=c, sq=sq: e.activation(
                    sq[:], Xc(c, base, base + TB), AF.Square), SC)
                S.wait("tensor", SC, t_sq)
                t_sqmm[c] = S.op("tensor", lambda e, sq=sq, c=c: e.matmul(
                    pS2, ones_cb_sb[:], sq[:],
                    start=(c == 0), stop=(c == DC - 1)), PE)
            t_s2 = PE.n
            S.wait("vector", PE, t_s1)
            S.op("vector", lambda e: e.tensor_scalar(row(R_M), pS1, 1.0 / D, None, ALU.mult), VE)
            S.wait("vector", PE, t_s2)
            t_ps2r = S.op("vector", lambda e: e.tensor_scalar(row(R_V), pS2, 1.0 / D, None, ALU.mult), VE)
            guard["ps1"] = (VE, t_ps2r)
            guard["ps2"] = (VE, t_ps2r)
            S.drain("vector")
            S.op("vector", lambda e: e.tensor_tensor(row(R_A1), row(R_M), row(R_M), ALU.mult), VE)
            S.drain("vector")
            t_v = S.op("vector", lambda e: e.tensor_tensor(row(R_V), row(R_V), row(R_A1), ALU.subtract), VE)
            S.wait("scalar", VE, t_v)
            if want_rms:
                S.op("scalar", lambda e: e.activation(row(R_A1), row(R_V), AF.Ln,
                                                      scale=cr(C_1PE2), bias=cr(C_E1E2)), SC)
                S.drain("scalar")
                S.op("scalar", lambda e: e.activation(row(R_RSRG), row(R_A1), AF.Exp, scale=cr(C_NHALF)), SC)
                S.drain("scalar")
                t_ln2r = S.op("scalar", lambda e: e.activation(row(R_MRSRG), row(R_V), AF.Ln, bias=cr(C_E1)), SC)
                S.wait("vector", SC, t_ln2r)
                t_sub = S.op("vector", lambda e: e.tensor_tensor(row(R_A2), row(R_A1), row(R_MRSRG), ALU.subtract), VE)
                S.wait("scalar", VE, t_sub)
                t_invrg = S.op("scalar", lambda e: e.activation(row(R_A1), row(R_A2), AF.Exp, scale=cr(C_HALF)), SC)
                S.wait("vector", SC, t_invrg)
                S.op("vector", lambda e: e.tensor_tensor(row(R_MRSRG), row(R_M), row(R_RSRG), ALU.mult), VE)
            else:
                S.op("scalar", lambda e: e.activation(row(R_A1), row(R_V), AF.Ln, bias=cr(C_E1)), SC)
                S.drain("scalar")
                t_rows_s = S.op("scalar", lambda e: e.activation(row(R_RSRG), row(R_A1), AF.Exp, scale=cr(C_NHALF)), SC)
                S.wait("vector", SC, t_rows_s)
                S.op("vector", lambda e: e.tensor_tensor(row(R_MRSRG), row(R_M), row(R_RSRG), ALU.mult), VE)
            S.drain("vector")
            return VE.n

        def ffn_sublayer(idx):
            S = Sched()
            sbl = ExitStack()
            mk = lambda name, shape, dt: sbl.enter_context(nc.sbuf_tensor(f"{name}_{idx}", shape, dt))
            sq_ring = [mk(f"sq{i}", [128, TB], BF16) for i in range(2)]
            y32_ring = [mk(f"y32_{i}", [128, TB], F32) for i in range(2)]
            tmp2_ring = [mk(f"tmp2_{i}", [128, TB], F32) for i in range(2)]
            Lsb = mk("Lsb", [8, TB], F32)
            Ltok = mk("Ltok", [128, QT * 8], F32)
            Msk = mk("Msk", [128, QT * 8], F32)
            M1 = mk("M1", [128, QT * 8], F32)
            M2 = mk("M2", [128, QT * 8], F32)
            gmax1 = mk("gmax1", [128, QT], F32)
            gmax2 = mk("gmax2", [128, QT], F32)
            gw = mk("gw", [128, 2 * QT], F32)
            C_tok = mk("C_tok", [128, QT * 8], BF16)
            Cexp = mk("Cexp", [8, T], BF16)
            cbrow_t = mk("cbrow", [1, 2 * T], BF16)
            cbrow = [cbrow_t[0:1, 0:T], cbrow_t[0:1, T:2 * T]]
            cb_sb = [mk(f"cb{i}", [128, T], BF16) for i in range(2)]
            gu_ring = [mk(f"gu{i}", [128, 2048], BF16) for i in range(NGU)]
            dn_ring = [mk(f"dn{i}", [128, 8 * 128], BF16) for i in range(NDN)]
            hid_sb = [mk(f"hid{i}", [128, T], BF16) for i in range(max(GROUPS))]
            sg_ring = [mk(f"sg{i}", [128, TH], BF16) for i in range(2)]
            tmpd_ring = [mk(f"tmpd{i}", [128, SW], F32) for i in range(2)]

            ps = ExitStack()
            pS1 = psum(ps, f"pS1_{idx}", [1, TB], F32)[0:1, 0:TB]
            pS2 = psum(ps, f"pS2_{idx}", [1, TB], F32)[0:1, 0:TB]
            pLg = psum(ps, f"pLg_{idx}", [8, TB], F32)[0:8, 0:TB]
            pB1 = psum(ps, f"pB1_{idx}", [128, TB], F32)[:, 0:TB]
            pB2 = psum(ps, f"pB2_{idx}", [128, TB], F32)[:, 0:TB]
            pB3 = psum(ps, f"pB3_{idx}", [128, TB], F32)[:, 0:TB]
            pLt = psum(ps, f"pLt_{idx}", [128, 8], F32)[:, 0:8]
            pCt = psum(ps, f"pCt_{idx}", [8, 128], BF16)[0:8, 0:128]

            guard = {}
            t_y_last = 0
            t_y_nb = {}
            t_ltok = 0
            for nb in range(NB):
                base = nb * TB
                t_rows = norm_stats_rows(S, base, pS1, pS2, sq_ring, True, guard)
                S.wait("tensor", VE, t_rows)
                S.waitt("tensor", guard.get("pB"))
                t_b1 = S.op("tensor", lambda e: e.matmul(pB1, ones_r_sb[:], row(R_RSRG), start=True, stop=True), PE)
                t_b2 = S.op("tensor", lambda e: e.matmul(pB2, ones_r_sb[:], row(R_MRSRG), start=True, stop=True), PE)
                t_b3 = S.op("tensor", lambda e: e.matmul(pB3, halves_r_sb[:], row(R_A1), start=True, stop=True), PE)
                S.waitt("tensor", guard.get("pLg"))
                t_router = 0
                lo, hi = base, base + TB
                for cp in range(0, DC, 2):
                    cs = (cp, cp + 1)
                    for c in cs:
                        S.waitt("vector", guard.get(("y32", c % 2)))
                        S.waitt("vector", guard.get(("y32pe", c % 2)))
                    S.wait("vector", PE, t_b2)
                    S.drain("vector")
                    for c in cs:
                        S.op("vector", lambda e, c=c, lo=lo, hi=hi: e.tensor_tensor(
                            y32_ring[c % 2][:], Xc(c, lo, hi), pB1, ALU.mult), VE)
                    S.drain("vector")
                    for c in cs:
                        t_y32 = S.op("vector", lambda e, c=c: e.tensor_tensor(
                            y32_ring[c % 2][:], y32_ring[c % 2][:], pB2, ALU.subtract), VE)
                    S.wait("scalar", VE, t_y32)
                    for c in cs:
                        t_y_last = S.op("scalar", lambda e, c=c, lo=lo, hi=hi: e.activation(
                            Yc(c, lo, hi), y32_ring[c % 2][:], AF.Copy,
                            scale=nw_sb[:, c:c + 1]), SC)
                        guard[("y32", c % 2)] = (SC, t_y_last)
                    S.wait("tensor", VE, t_y32)
                    for c in cs:
                        t_router = S.op("tensor", lambda e, c=c: e.matmul(
                            pLg, rtr2_sb[:, c * E:(c + 1) * E], y32_ring[c % 2][:],
                            start=(c == 0), stop=(c == DC - 1)), PE)
                        guard[("y32pe", c % 2)] = (PE, t_router)
                    S.wait("vector", PE, t_b3)
                    S.drain("vector")
                    for c in cs:
                        S.op("vector", lambda e, c=c: e.tensor_tensor(
                            tmp2_ring[c % 2][:], y32_ring[c % 2][:], pB3, ALU.mult), VE)
                    S.drain("vector")
                    for c in cs:
                        S.op("vector", lambda e, c=c, lo=lo, hi=hi: e.tensor_tensor(
                            Xc(c, lo, hi), Xc(c, lo, hi), tmp2_ring[c % 2][:], ALU.add), VE)
                guard["pB"] = (VE, VE.n)
                t_y_nb[nb] = t_y_last
                S.wait("scalar", PE, t_router)
                t_lsb = S.op("scalar", lambda e: e.copy(Lsb[:], pLg), SC)
                guard["pLg"] = (SC, t_lsb)
                for j in range(TB // 128):
                    S.wait("tensor", SC, t_lsb)
                    if t_ltok:
                        S.wait("tensor", VE, t_ltok)
                    t_lt = S.op("tensor", lambda e, j=j: e.transpose(
                        pLt, Lsb[:, j * 128:(j + 1) * 128], idf_sb[:]), PE)
                    S.wait("vector", PE, t_lt)
                    q = nb * (TB // 128) + j
                    t_ltok = S.op("vector", lambda e, q=q: e.tensor_copy(
                        Ltok[:, q * 8:(q + 1) * 8], pLt), VE)

            L3 = Ltok[:].rearrange("p (g o) -> p g o", o=8)
            M3 = lambda t: t[:].rearrange("p (g o) -> p g o", o=8)
            g1b = gmax1[:].rearrange("p (g o) -> p g o", o=1).broadcast_to([128, QT, 8])
            g2b = gmax2[:].rearrange("p (g o) -> p g o", o=1).broadcast_to([128, QT, 8])
            S.drain("vector")
            S.op("vector", lambda e: e.reduce_max(
                gmax1[:].rearrange("p (g o) -> p g o", o=1), L3, axis=AX.X), VE)
            S.drain("vector")
            S.op("vector", lambda e: e.tensor_tensor(M3(M1), L3, g1b, ALU.is_ge), VE)
            S.drain("vector")
            S.op("vector", lambda e: e.scalar_tensor_tensor(
                Msk[:], M1[:], NEG_BIG, Ltok[:], ALU.mult, ALU.add), VE)
            S.drain("vector")
            S.op("vector", lambda e: e.reduce_max(
                gmax2[:].rearrange("p (g o) -> p g o", o=1), M3(Msk), axis=AX.X), VE)
            S.drain("vector")
            S.op("vector", lambda e: e.tensor_tensor(M3(M2), M3(Msk), g2b, ALU.is_ge), VE)
            t_d21 = S.op("vector", lambda e: e.tensor_tensor(
                gw[:, QT:2 * QT], gmax2[:], gmax1[:], ALU.subtract), VE)
            S.wait("scalar", VE, t_d21)
            t_w2 = S.op("scalar", lambda e: e.activation(
                gw[:, QT:2 * QT], gw[:, QT:2 * QT], AF.Sigmoid), SC)
            S.wait("vector", SC, t_w2)
            S.op("vector", lambda e: e.tensor_scalar(
                gw[:, 0:QT], gw[:, QT:2 * QT], -1.0, 1.0, ALU.mult, ALU.add), VE)
            S.drain("vector")
            w1b = gw[:, 0:QT].rearrange("p (g o) -> p g o", o=1).broadcast_to([128, QT, 8])
            w2b = gw[:, QT:2 * QT].rearrange("p (g o) -> p g o", o=1).broadcast_to([128, QT, 8])
            S.op("vector", lambda e: e.tensor_tensor(M3(M1), M3(M1), w1b, ALU.mult), VE)
            S.op("vector", lambda e: e.tensor_tensor(M3(M2), M3(M2), w2b, ALU.mult), VE)
            S.drain("vector")
            S.op("vector", lambda e: e.tensor_tensor(M1[:], M1[:], M2[:], ALU.add), VE)
            S.drain("vector")
            t_ctok = S.op("vector", lambda e: e.tensor_scalar(
                C_tok[:], M1[:], ha_sb[:], None, ALU.mult), VE)
            t_cexp = 0
            for q in range(QT):
                S.wait("tensor", VE, t_ctok)
                if t_cexp:
                    S.wait("tensor", VE, t_cexp)
                t_ct = S.op("tensor", lambda e, q=q: e.transpose(
                    pCt, C_tok[:, q * 8:(q + 1) * 8], idb_sb[:]), PE)
                S.wait("vector", PE, t_ct)
                t_cexp = S.op("vector", lambda e, q=q: e.tensor_copy(
                    Cexp[:, q * 128:(q + 1) * 128], pCt), VE)

            ps.close()
            ps2 = ExitStack()
            GU = [psum(ps2, f"pGU{i}_{idx}", [128, TH], F32)[:, 0:TH] for i in range(4)]
            gu_fr = {i: None for i in range(4)}
            slot_gu = {}
            slot_dn = {}
            cb_fr = {}
            cbrow_fr = {}
            t_cb_ready = {}
            t_bcast_prev = 0
            for e_i in range(E):
                sl = e_i % 2
                S.wait("sync", VE, t_cexp)
                S.wait("sync", PE, t_bcast_prev)
                t_cbrow = S.op("sync", lambda e, e_i=e_i, sl=sl: e.dma_start(
                    cbrow[sl], Cexp[e_i:e_i + 1, :]), DS)

                def cb_broadcast(sl=sl, t_cbrow=t_cbrow):
                    for j in range(T // SW):
                        S.wait("tensor", DS, t_cbrow)
                        S.waitt("tensor", gu_fr[0])
                        S.waitt("tensor", cb_fr.get(sl))
                        t_bc = S.op("tensor", lambda e, j=j, sl=sl: e.matmul(
                            GU[0][:, 0:SW], ones_rb_sb[:], cbrow[sl][:, j * SW:(j + 1) * SW],
                            start=True, stop=True), PE)
                        S.wait("scalar", PE, t_bc)
                        t_cbj = S.op("scalar", lambda e, j=j, sl=sl: e.copy(
                            cb_sb[sl][:, j * SW:(j + 1) * SW], GU[0][:, 0:SW]), SC)
                        gu_fr[0] = (SC, t_cbj)
                    cbrow_fr[sl] = (PE, PE.n)
                    t_cb_ready[sl] = SC.n

                if e_i == 0:
                    S.wait("tensor", SC, t_y_last)
                    S.wait("tensor", VE, t_cexp)

                cb_done = False
                k0 = 0
                for g_i, gsz in enumerate(GROUPS):
                    t_hid = {}
                    for i_l in range(gsz):
                        mm_i = k0 + i_l
                        slot = (e_i * NPAIR + mm_i) % NGU
                        S.waitt("gpsimd", slot_gu.get(slot))
                        t_w = S.op("gpsimd", lambda e, e_i=e_i, mm_i=mm_i, slot=slot: e.dma_start(
                            gu_ring[slot][:], gup.ap()[e_i, mm_i]), DWS[slot])
                        t_gate_nh = {}
                        t_up_nh = {}
                        S.wait("tensor", DWS[slot], t_w)
                        for nh in range(NH):
                            S.waitt("tensor", gu_fr[nh])
                            for ns in range(NS):
                                S.wait("tensor", SC, t_y_nb[(nh * TH + ns * SW) // TB])
                                for k in range(DC):
                                    S.op("tensor", lambda e, k=k, ns=ns, nh=nh, slot=slot: e.matmul(
                                        GU[nh][:, ns * SW:(ns + 1) * SW],
                                        gu_ring[slot][:, k * 256: k * 256 + 128],
                                        Yc(k, nh * TH + ns * SW, nh * TH + (ns + 1) * SW),
                                        start=(k == 0), stop=(k == DC - 1)),
                                        PE if (k == DC - 1 and ns == NS - 1) else None)
                            t_gate_nh[nh] = PE.n
                        for nh in range(NH):
                            S.waitt("tensor", gu_fr[2 + nh])
                            for ns in range(NS):
                                for k in range(DC):
                                    S.op("tensor", lambda e, k=k, ns=ns, nh=nh, slot=slot: e.matmul(
                                        GU[2 + nh][:, ns * SW:(ns + 1) * SW],
                                        gu_ring[slot][:, k * 256 + 128: k * 256 + 256],
                                        Yc(k, nh * TH + ns * SW, nh * TH + (ns + 1) * SW),
                                        start=(k == 0), stop=(k == DC - 1)),
                                        PE if (k == DC - 1 and ns == NS - 1) else None)
                            t_up_nh[nh] = PE.n
                        t_pair = PE.n
                        slot_gu[slot] = (PE, t_pair)
                        for nh in range(NH):
                            sg = sg_ring[nh]
                            S.wait("scalar", PE, t_gate_nh[nh])
                            S.waitt("scalar", guard.get(("sg", nh)))
                            t_sg = S.op("scalar", lambda e, nh=nh, sg=sg: e.activation(
                                sg[:], GU[nh], AF.Silu), SC)
                            gu_fr[nh] = (SC, t_sg)
                            S.wait("vector", SC, t_sg)
                            S.wait("vector", PE, t_up_nh[nh])
                            t_h = S.op("vector", lambda e, i_l=i_l, nh=nh, sg=sg: e.tensor_tensor(
                                hid_sb[i_l][:, nh * TH:(nh + 1) * TH], sg[:], GU[2 + nh], ALU.mult), VE)
                            t_hid[(i_l, nh)] = t_h
                            gu_fr[2 + nh] = (VE, t_h)
                            guard[("sg", nh)] = (VE, t_h)
                    if not cb_done:
                        cb_broadcast()
                        t_bcast_prev = PE.n
                        cb_done = True
                    for md in range(NMD):
                        slot = (e_i * len(GROUPS) * NMD + g_i * NMD + md) % NDN
                        S.waitt("gpsimd", slot_dn.get(slot))
                        t_wd = S.op("gpsimd", lambda e, e_i=e_i, md=md, k0=k0, gsz=gsz, slot=slot: e.dma_start(
                            dn_ring[slot][:, 0:gsz * 128],
                            dwn.ap()[e_i, md][:, k0 * 128:(k0 + gsz) * 128]), DWS[NGU + slot])
                        dpar = (md % 2) * 2
                        for nh in range(NH):
                            S.waitt("tensor", gu_fr[dpar + nh])
                        S.wait("tensor", DWS[NGU + slot], t_wd)
                        for k in range(gsz):
                            lastk = (k == gsz - 1)
                            S.wait("tensor", VE, t_hid[(k, NH - 1)])
                            for nh in range(NH):
                                for ns in range(NS):
                                    S.op("tensor", lambda e, k=k, ns=ns, nh=nh, slot=slot, dpar=dpar, gsz=gsz: e.matmul(
                                        GU[dpar + nh][:, ns * SW:(ns + 1) * SW],
                                        dn_ring[slot][:, k * 128:(k + 1) * 128],
                                        hid_sb[k][:, nh * TH + ns * SW: nh * TH + (ns + 1) * SW],
                                        start=(k == 0), stop=(k == gsz - 1)),
                                        PE if (lastk and nh == NH - 1 and ns == NS - 1) else None)
                        t_dmm = PE.n
                        slot_dn[slot] = (PE, t_dmm)
                        S.wait("vector", PE, t_dmm)
                        S.wait("vector", SC, t_cb_ready[sl])
                        t_td = 0
                        for nh in range(NH):
                            for ns in range(NS):
                                td = tmpd_ring[ns % 2]
                                S.drain("vector")
                                S.op("vector", lambda e, td=td, ns=ns, nh=nh, dpar=dpar, sl=sl: e.tensor_tensor(
                                    td[:], GU[dpar + nh][:, ns * SW:(ns + 1) * SW],
                                    cb_sb[sl][:, nh * TH + ns * SW: nh * TH + (ns + 1) * SW],
                                    ALU.mult), VE)
                                S.drain("vector")
                                t_td = S.op("vector", lambda e, td=td, ns=ns, nh=nh, md=md: e.tensor_tensor(
                                    Xc(md, nh * TH + ns * SW, nh * TH + (ns + 1) * SW),
                                    Xc(md, nh * TH + ns * SW, nh * TH + (ns + 1) * SW),
                                    td[:], ALU.add), VE)
                        gu_fr[dpar] = (VE, t_td)
                        gu_fr[dpar + 1] = (VE, t_td)
                        cb_fr[sl] = (VE, t_td)
                    k0 += gsz

            S.emit(nc, f"ffn{idx}")
            ps2.close()
            sbl.close()

        def attn_sublayer():
            S = Sched()
            sbl = ExitStack()
            mk = lambda name, shape, dt: sbl.enter_context(nc.sbuf_tensor(f"{name}_a", shape, dt))
            sq_ring = [mk(f"sq{i}", [128, TB], BF16) for i in range(2)]
            a32_ring = [mk("a32", [128, TB], F32)] * 2
            v_sb = mk("v_sb", [128, QT * 1024], BF16)
            v_fm = mk("v_fm", [128, T], BF16)
            fm = mk("fm", [128, H * T], BF16)
            q_sb = mk("q_sb", [128, T], BF16)
            k_sb = mk("k_sb", [128, T], BF16)
            p_sb = mk("p_sb", [128, T], BF16)
            pT_sb = [mk(f"pT{i}", [128, 128], BF16) for i in range(3)]
            o_tok = mk("o_tok", [128, 256], BF16)
            lqlk = mk("lqlk", [1, 2 * T], BF16)
            lqr = lqlk[0:1, 0:T]
            lkr = lqlk[0:1, T:2 * T]
            nmax = mk("nmax", [128, 2], F32)
            bmax = mk("bmax", [128, 4], F32)
            rsum = mk("rsum", [128, 8], F32)
            rcp = mk("rcp", [128, 2], F32)
            wcmb = mk("wcmb", [128, 4], F32)
            ctmp16 = mk("ctmp16", [128, 16], F32)
            ocmb = a32_ring[0]
            wqk_ring = [mk(f"wqk{i}", [128, 1024], BF16) for i in range(3)]
            wo_ring = wqk_ring

            psn = ExitStack()
            pS1 = psum(psn, "pS1_a", [1, TB], F32)[0:1, 0:TB]
            pS2 = psum(psn, "pS2_a", [1, TB], F32)[0:1, 0:TB]
            pB1 = psum(psn, "pB1_a", [128, TB], F32)[:, 0:TB]
            pB2 = psum(psn, "pB2_a", [128, TB], F32)[:, 0:TB]
            guard = {}
            for nb in range(NB):
                base = nb * TB
                t_rows = norm_stats_rows(S, base, pS1, pS2, sq_ring, False, guard)
                S.wait("tensor", VE, t_rows)
                S.waitt("tensor", guard.get("pB"))
                t_b1 = S.op("tensor", lambda e: e.matmul(pB1, ones_r_sb[:], row(R_RSRG), start=True, stop=True), PE)
                t_b2 = S.op("tensor", lambda e: e.matmul(pB2, ones_r_sb[:], row(R_MRSRG), start=True, stop=True), PE)
                for c in range(DC):
                    a32 = a32_ring[c % 2]
                    lo, hi = base, base + TB
                    S.wait("vector", PE, t_b2)
                    S.drain("vector")
                    S.op("vector", lambda e, c=c, a32=a32, lo=lo, hi=hi: e.tensor_tensor(
                        a32[:], Xc(c, lo, hi), pB1, ALU.mult), VE)
                    S.drain("vector")
                    S.op("vector", lambda e, c=c, a32=a32, lo=lo, hi=hi: e.tensor_tensor(
                        Yc(c, lo, hi), a32[:], pB2, ALU.subtract), VE)
                guard["pB"] = (VE, VE.n)
            t_ln2 = VE.n
            psn.close()

            ps = ExitStack()
            pS = psum(ps, "pS_a", [128, min(2048, T)], F32)
            pQK = psum(ps, "pQK_a", [128, 512], F32)
            pAV = psum(ps, "pAV_a", [128, 512], F32)[:, 0:512]
            pTr2 = [psum(ps, f"pTr{i}_a", [128, 128], BF16)[:, 0:128] for i in range(2)]

            pqk_fr = None
            wqk_fr = {}
            qk_buf_fr = {0: None, 1: None}
            ps_rd = {0: None, 1: None}
            t_lrow_fr = {}
            tr_fr = {}
            pav_fr = {}
            pt_slot_fr = {}
            nmax_fr = {}
            rsum_fr = {}
            otok_fr = {}
            w_fr = {}
            ct_fr = {}
            gslot = [0]
            prev_te = [0]
            t_fm_last = 0
            v_fm_fr = None
            for h in range(H):
                for sel in range(3):
                    S.waitt("gpsimd", wqk_fr.get(sel))
                    t_wq = S.op("gpsimd", lambda e, h=h, sel=sel: e.dma_start(
                        wqk_ring[sel][:], wqk.ap()[sel, h]), DWS[NGU + NDN + sel])
                    dst = [q_sb, k_sb, v_fm][sel]
                    for nh in range(NH):
                        for ns in range(NS):
                            lo = nh * TH + ns * SW
                            S.wait("tensor", DWS[NGU + NDN + sel], t_wq)
                            if h == 0:
                                S.wait("tensor", VE, t_ln2)
                            S.waitt("tensor", pqk_fr)
                            for k in range(DC):
                                S.op("tensor", lambda e, k=k, lo=lo, sel=sel: e.matmul(
                                    pQK[:, 0:SW],
                                    wqk_ring[sel][:, k * 128:(k + 1) * 128],
                                    Yc(k, lo, lo + SW),
                                    start=(k == 0), stop=(k == DC - 1)),
                                    PE if k == DC - 1 else None)
                            t_mm = PE.n
                            if sel == 0:
                                S.wait("scalar", PE, t_mm)
                                S.waitt("scalar", qk_buf_fr[0])
                                t_ev = S.op("scalar", lambda e, lo=lo: e.activation(
                                    q_sb[:, lo:lo + SW], pQK[:, 0:SW], AF.Copy,
                                    scale=cc(C_ISQH)), SC)
                                pqk_fr = (SC, t_ev)
                            else:
                                S.wait("vector", PE, t_mm)
                                S.waitt("vector", qk_buf_fr[1] if sel == 1 else v_fm_fr)
                                t_ev = S.op("vector", lambda e, lo=lo, dst=dst: e.tensor_copy(
                                    dst[:, lo:lo + SW], pQK[:, 0:SW]), VE)
                                pqk_fr = (VE, t_ev)
                    wqk_fr[sel] = (PE, PE.n)
                t_vfm_ev = VE.n
                for tt in range(QT):
                    ptr = pTr2[tt % 2]
                    S.wait("tensor", VE, t_vfm_ev)
                    S.waitt("tensor", tr_fr.get(tt % 2))
                    t_vt = S.op("tensor", lambda e, tt=tt, ptr=ptr: e.transpose(
                        ptr, v_fm[:, tt * 128:(tt + 1) * 128], idb_sb[:]), PE)
                    S.wait("vector", PE, t_vt)
                    t_ev = S.op("vector", lambda e, tt=tt, h=h, ptr=ptr: e.tensor_copy(
                        v_sb[:, tt * 1024 + h * 128: tt * 1024 + (h + 1) * 128], ptr), VE)
                    tr_fr[tt % 2] = (VE, t_ev)
                v_fm_fr = (PE, PE.n)
                t_v_done = VE.n
                t_q_ready = SC.n
                t_k_ready = VE.n
                nrow = T // SW
                t_lrow_s = 0
                t_lrow_v = 0
                for sel in range(2):
                    src = q_sb if sel == 0 else k_sb
                    for j in range(nrow):
                        S.wait("tensor", SC, t_q_ready)
                        S.wait("tensor", VE, t_k_ready)
                        S.waitt("tensor", ps_rd[0])
                        S.waitt("tensor", ps_rd[1])
                        S.waitt("tensor", t_lrow_fr.get((sel ^ 1, j)))
                        S.waitt("tensor", t_lrow_fr.get((sel, j)))
                        t_r = S.op("tensor", lambda e, j=j, src=src, h=h: e.matmul(
                            pS[0:1, j * SW:(j + 1) * SW], dirs_sb[:, h:h + 1],
                            src[:, j * SW:(j + 1) * SW], start=True, stop=True), PE)
                        if sel == 0:
                            S.wait("scalar", PE, t_r)
                            t_lrow_s = S.op("scalar", lambda e, j=j, h=h: e.activation(
                                lqr[:, j * SW:(j + 1) * SW], pS[0:1, j * SW:(j + 1) * SW],
                                AF.Copy, scale=hs2_sb[0:1, h:h + 1]), SC)
                            t_lrow_fr[(sel, j)] = (SC, t_lrow_s)
                        else:
                            S.wait("vector", PE, t_r)
                            t_lrow_v = S.op("vector", lambda e, j=j: e.tensor_copy(
                                lkr[:, j * SW:(j + 1) * SW], pS[0:1, j * SW:(j + 1) * SW]), VE)
                            t_lrow_fr[(sel, j)] = (VE, t_lrow_v)
                subs = []
                sub_groups = {}
                for t_i in range(QT):
                    KT = (t_i + 1) * 128
                    cuts = [(0, KT)] if KT <= TH else [(0, TH), (TH, KT)]
                    for si, (lo, hi) in enumerate(cuts):
                        sub_groups.setdefault(t_i, []).append(len(subs))
                        subs.append((t_i, si, lo, hi))
                nsub = len(subs)
                geo_t = {}
                expb = {}
                avst = {}
                pending_o = []

                def emit_scores(sg):
                    t_i, si, lo, hi = subs[sg]
                    par = sg % 2
                    ws = hi - lo
                    nb = (ws + SW - 1) // SW
                    S.wait("tensor", SC, t_lrow_s)
                    S.wait("tensor", VE, max(t_lrow_v, t_v_done))
                    S.waitt("tensor", ps_rd.get(par))
                    tg = {}
                    for b in range(nb):
                        w = min(SW, ws - b * SW)
                        off = par * TH + b * SW
                        S.op("tensor", lambda e, t_i=t_i, lo=lo, b=b, w=w, off=off: e.matmul(
                            pS[:, off: off + w],
                            q_sb[:, t_i * 128:(t_i + 1) * 128],
                            k_sb[:, lo + b * SW: lo + b * SW + w],
                            start=True, stop=False), None)
                    for b in range(nb):
                        w = min(SW, ws - b * SW)
                        off = par * TH + b * SW
                        tg[b] = S.op("tensor", lambda e, t_i=t_i, lo=lo, b=b, w=w, off=off: e.matmul(
                            pS[:, off: off + w],
                            lqr[:, t_i * 128:(t_i + 1) * 128],
                            lkr[:, lo + b * SW: lo + b * SW + w],
                            start=False, stop=True), PE)
                    geo_t[sg] = tg

                def emit_softmax(sg, te_prev):
                    t_i, si, lo, hi = subs[sg]
                    par = sg % 2
                    ws = hi - lo
                    nb = (ws + SW - 1) // SW
                    last = (hi == (t_i + 1) * 128)
                    for b in range(nb):
                        w = min(SW, ws - b * SW)
                        off = par * TH + b * SW
                        S.wait("vector", PE, geo_t[sg][b])
                        if last and b == nb - 1:
                            doff = par * TH + (t_i * 128 - lo)
                            S.op("vector", lambda e, doff=doff: e.tensor_tensor(
                                pS[:, doff:doff + 128],
                                pS[:, doff:doff + 128], cmask_sb[:], ALU.add), VE)
                            S.drain("vector")
                        S.op("vector", lambda e, b=b, w=w, off=off, par=par: e.reduce_max(
                            bmax[:, par * 2 + b: par * 2 + b + 1],
                            pS[:, off: off + w], axis=AX.X), VE)
                    S.drain("vector")
                    S.waitt("vector", nmax_fr.get(par))
                    t_m = S.op("vector", lambda e, par=par, nb=nb: e.reduce_max(
                        nmax[:, par:par + 1], bmax[:, par * 2: par * 2 + nb],
                        axis=AX.X, negate=True), VE)
                    S.wait("scalar", VE, t_m)
                    S.wait("scalar", PE, te_prev)
                    S.waitt("scalar", rsum_fr.get(par))
                    eb = {}
                    ncb = (ws + 255) // 256
                    for cb in range(ncb):
                        w = min(256, ws - cb * 256)
                        off = par * TH + cb * 256
                        eb[cb] = S.op("scalar", lambda e, lo=lo, cb=cb, w=w, off=off, par=par: e.activation(
                            p_sb[:, lo + cb * 256: lo + cb * 256 + w],
                            pS[:, off: off + w], AF.Exp,
                            bias=nmax[:, par:par + 1],
                            accum_out=rsum[0:128, par * 4 + cb: par * 4 + cb + 1]), SC)
                    expb[sg] = eb
                    ps_rd[par] = (SC, eb[ncb - 1])
                    nmax_fr[par] = (SC, eb[ncb - 1])

                def emit_trans_av(sg):
                    t_i, si, lo, hi = subs[sg]
                    ws = hi - lo
                    ntp = ws // 128
                    r = (t_i % 2) * 2 + si
                    pt_ev = {}
                    pt_slot = {}

                    def do_av(j, h=h):
                        S.waitt("tensor", pt_ev[j])
                        if j == 0:
                            S.waitt("tensor", pav_fr.get(r))
                        kb = lo // 128 + j
                        S.op("tensor", lambda e, j=j, kb=kb, r=r, sl=pt_slot[j], ntp=ntp, h=h: e.matmul(
                            pAV[:, r * 128:(r + 1) * 128], pT_sb[sl][:],
                            v_sb[:, kb * 1024 + h * 128: kb * 1024 + (h + 1) * 128],
                            start=(j == 0), stop=(j == ntp - 1)),
                            PE if j == ntp - 1 else None)
                        pt_slot_fr[pt_slot[j]] = (PE, PE.n)

                    for j in range(ntp):
                        slot = gslot[0] % 3
                        trp = gslot[0] % 2
                        gslot[0] += 1
                        pt_slot[j] = slot
                        ptr = pTr2[trp]
                        S.wait("tensor", SC, expb[sg][(j * 128) // 256])
                        S.waitt("tensor", tr_fr.get(trp))
                        t_trm = S.op("tensor", lambda e, lo=lo, j=j, ptr=ptr: e.transpose(
                            ptr, p_sb[:, lo + j * 128: lo + (j + 1) * 128], idb_sb[:]), PE)
                        S.wait("vector", PE, t_trm)
                        S.waitt("vector", pt_slot_fr.get(slot))
                        t_ev = S.op("vector", lambda e, sl=slot, ptr=ptr: e.tensor_copy(
                            pT_sb[sl][:], ptr), VE)
                        tr_fr[trp] = (VE, t_ev)
                        pt_ev[j] = (VE, t_ev)
                        if j >= 2:
                            do_av(j - 2)
                    for j in range(max(0, ntp - 2), ntp):
                        do_av(j)
                    avst[sg] = PE.n
                    return PE.n

                def emit_combine(t_i):
                    sgs = sub_groups[t_i]
                    tp = t_i % 2
                    eb_last = max(expb[sg][len(expb[sg]) - 1] for sg in sgs)
                    S.wait("vector", SC, eb_last)
                    sv = []
                    for idx, sg in enumerate(sgs):
                        par = sg % 2
                        ncb = len(expb[sg])
                        if ncb >= 2:
                            sc_col = ctmp16[:, tp * 8 + 1 + idx: tp * 8 + 2 + idx]
                            S.drain("vector")
                            S.op("vector", lambda e, par=par, sc_col=sc_col: e.tensor_tensor(
                                sc_col, rsum[:, par * 4: par * 4 + 1],
                                rsum[:, par * 4 + 1: par * 4 + 2], ALU.add), VE)
                            for k in range(2, ncb):
                                S.drain("vector")
                                S.op("vector", lambda e, par=par, k=k, sc_col=sc_col: e.tensor_tensor(
                                    sc_col, sc_col,
                                    rsum[:, par * 4 + k: par * 4 + k + 1], ALU.add), VE)
                            sv.append(sc_col)
                        else:
                            sv.append(rsum[:, par * 4: par * 4 + 1])
                    if len(sgs) == 1:
                        par = sgs[0] % 2
                        S.drain("vector")
                        S.op("vector", lambda e, tp=tp, s0=sv[0]: e.reciprocal(
                            rcp[:, tp:tp + 1], s0), VE)
                        S.drain("vector")
                        S.wait("vector", PE, avst[sgs[0]])
                        S.waitt("vector", otok_fr.get(tp))
                        r0 = tp * 2
                        t_ot = S.op("vector", lambda e, tp=tp, r0=r0: e.tensor_scalar(
                            o_tok[:, tp * 128:(tp + 1) * 128],
                            pAV[:, r0 * 128:(r0 + 1) * 128],
                            rcp[:, tp:tp + 1], None, ALU.mult), VE)
                        pav_fr[r0] = (VE, t_ot)
                        rsum_fr[par] = (VE, t_ot)
                    else:
                        sp0, sp1 = sgs[0] % 2, sgs[1] % 2
                        S.drain("vector")
                        S.op("vector", lambda e, tp=tp: e.tensor_tensor(
                            ctmp16[:, tp * 8: tp * 8 + 1],
                            nmax[:, 0:1], nmax[:, 1:2], ALU.min), VE)
                        S.drain("vector")
                        S.waitt("vector", ct_fr.get(tp))
                        S.op("vector", lambda e, tp=tp: e.tensor_scalar(
                            ctmp16[:, tp * 8 + 6: tp * 8 + 8], nmax[:, 0:2],
                            ctmp16[:, tp * 8: tp * 8 + 1], None, ALU.subtract), VE)
                        S.drain("vector")
                        t_dd = S.op("vector", lambda e, tp=tp: e.tensor_scalar(
                            ctmp16[:, tp * 8 + 6: tp * 8 + 8],
                            ctmp16[:, tp * 8 + 6: tp * 8 + 8],
                            -1.0, None, ALU.mult), VE)
                        S.wait("scalar", VE, t_dd)
                        S.waitt("scalar", w_fr.get(tp))
                        t_w = S.op("scalar", lambda e, tp=tp: e.activation(
                            wcmb[:, tp * 2: tp * 2 + 2],
                            ctmp16[:, tp * 8 + 6: tp * 8 + 8], AF.Exp), SC)
                        ct_fr[tp] = (SC, t_w)
                        S.wait("vector", SC, t_w)
                        S.drain("vector")
                        S.op("vector", lambda e, tp=tp, sp0=sp0, s0=sv[0]: e.tensor_tensor(
                            ctmp16[:, tp * 8 + 3: tp * 8 + 4], s0,
                            wcmb[:, tp * 2 + sp0: tp * 2 + sp0 + 1], ALU.mult), VE)
                        S.op("vector", lambda e, tp=tp, sp1=sp1, s1=sv[1]: e.tensor_tensor(
                            ctmp16[:, tp * 8 + 4: tp * 8 + 5], s1,
                            wcmb[:, tp * 2 + sp1: tp * 2 + sp1 + 1], ALU.mult), VE)
                        S.drain("vector")
                        S.op("vector", lambda e, tp=tp: e.tensor_tensor(
                            ctmp16[:, tp * 8 + 5: tp * 8 + 6],
                            ctmp16[:, tp * 8 + 3: tp * 8 + 4],
                            ctmp16[:, tp * 8 + 4: tp * 8 + 5], ALU.add), VE)
                        S.drain("vector")
                        S.op("vector", lambda e, tp=tp: e.reciprocal(
                            rcp[:, tp:tp + 1], ctmp16[:, tp * 8 + 5: tp * 8 + 6]), VE)
                        S.drain("vector")
                        S.op("vector", lambda e, tp=tp, sp0=sp0: e.tensor_tensor(
                            ctmp16[:, tp * 8 + 3: tp * 8 + 4],
                            wcmb[:, tp * 2 + sp0: tp * 2 + sp0 + 1],
                            rcp[:, tp:tp + 1], ALU.mult), VE)
                        t_c1 = S.op("vector", lambda e, tp=tp, sp1=sp1: e.tensor_tensor(
                            ctmp16[:, tp * 8 + 4: tp * 8 + 5],
                            wcmb[:, tp * 2 + sp1: tp * 2 + sp1 + 1],
                            rcp[:, tp:tp + 1], ALU.mult), VE)
                        w_fr[tp] = (VE, t_c1)
                        S.drain("vector")
                        S.wait("vector", PE, avst[sgs[1]])
                        r0, r1 = tp * 2, tp * 2 + 1
                        S.op("vector", lambda e, tp=tp, r0=r0: e.tensor_scalar(
                            ocmb[:, 0:128], pAV[:, r0 * 128:(r0 + 1) * 128],
                            ctmp16[:, tp * 8 + 3: tp * 8 + 4], None, ALU.mult), VE)
                        S.op("vector", lambda e, tp=tp, r1=r1: e.tensor_scalar(
                            ocmb[:, 128:256], pAV[:, r1 * 128:(r1 + 1) * 128],
                            ctmp16[:, tp * 8 + 4: tp * 8 + 5], None, ALU.mult), VE)
                        S.drain("vector")
                        S.waitt("vector", otok_fr.get(tp))
                        t_ot = S.op("vector", lambda e, tp=tp: e.tensor_tensor(
                            o_tok[:, tp * 128:(tp + 1) * 128],
                            ocmb[:, 0:128], ocmb[:, 128:256], ALU.add), VE)
                        pav_fr[r0] = (VE, t_ot)
                        pav_fr[r1] = (VE, t_ot)
                        rsum_fr[sp0] = (VE, t_ot)
                        rsum_fr[sp1] = (VE, t_ot)
                    pending_o.append((t_i, VE.n))

                def emit_o(t_done, t_ready):
                    tp = t_done % 2
                    trp = gslot[0] % 2
                    gslot[0] += 1
                    ptro = pTr2[trp]
                    S.wait("tensor", VE, t_ready)
                    S.waitt("tensor", tr_fr.get(trp))
                    t_otr = S.op("tensor", lambda e, tp=tp, ptro=ptro: e.transpose(
                        ptro, o_tok[:, tp * 128:(tp + 1) * 128], idb_sb[:]), PE)
                    otok_fr[tp] = (PE, t_otr)
                    S.wait("vector", PE, t_otr)
                    t_fme = S.op("vector", lambda e, h=h, t_done=t_done, ptro=ptro: e.tensor_copy(
                        fm[:, h * T + t_done * 128: h * T + (t_done + 1) * 128], ptro), VE)
                    tr_fr[trp] = (VE, t_fme)
                    return t_fme

                for s in range(nsub):
                    emit_scores(s)
                    if pending_o:
                        t_fm_last = emit_o(*pending_o.pop(0))
                    te = emit_trans_av(s - 1) if s > 0 else prev_te[0]
                    if s > 0:
                        pt_i, psi, plo, phi = subs[s - 1]
                        if phi == (pt_i + 1) * 128:
                            emit_combine(pt_i)
                    emit_softmax(s, te)
                emit_trans_av(nsub - 1)
                emit_combine(subs[nsub - 1][0])
                while pending_o:
                    t_fm_last = emit_o(*pending_o.pop(0))
                prev_te[0] = PE.n
                qk_buf_fr[0] = (PE, PE.n)
                qk_buf_fr[1] = (PE, PE.n)

            wo_fr = {0: wqk_fr.get(0), 1: wqk_fr.get(1)}
            op_fr = {0: ps_rd[0], 1: ps_rd[1]}
            for md in range(NMD):
                slot = md % 2
                S.waitt("gpsimd", wo_fr.get(slot))
                t_wo = S.op("gpsimd", lambda e, md=md, slot=slot: e.dma_start(
                    wo_ring[slot][:], wo.ap()[md]), DWS[NGU + NDN + slot])
                for nh in range(NH):
                    opar = (md * NH + nh) % 2
                    if 2 * TH <= min(2048, T):
                        dst = pS[:, opar * TH:(opar + 1) * TH]
                    else:
                        dst = pS[:, 0:TH]
                        opar = 0
                    S.wait("tensor", DWS[NGU + NDN + slot], t_wo)
                    S.wait("tensor", VE, t_fm_last)
                    S.waitt("tensor", op_fr[opar])
                    for k in range(DC):
                        last = (k == DC - 1)
                        for ns in range(NS):
                            S.op("tensor", lambda e, k=k, ns=ns, nh=nh, slot=slot, dst=dst: e.matmul(
                                dst[:, ns * SW:(ns + 1) * SW],
                                wo_ring[slot][:, k * 128:(k + 1) * 128],
                                fm[:, k * T + nh * TH + ns * SW: k * T + nh * TH + (ns + 1) * SW],
                                start=(k == 0), stop=(k == DC - 1)),
                                PE if (last and ns == NS - 1) else None)
                    t_omm = PE.n
                    S.wait("vector", PE, t_omm)
                    t_oadd = S.op("vector", lambda e, md=md, nh=nh, dst=dst: e.tensor_tensor(
                        Xc(md, nh * TH, (nh + 1) * TH),
                        Xc(md, nh * TH, (nh + 1) * TH), dst[:, 0:TH], ALU.add), VE)
                    op_fr[opar] = (VE, t_oadd)
                wo_fr[slot] = (PE, PE.n)

            S.emit(nc, "attn")
            ps.close()
            sbl.close()

        if 1 in parts:
            ffn_sublayer(1)
        if 2 in parts:
            attn_sublayer()
        if 3 in parts:
            ffn_sublayer(2)

        S = Sched()
        qs = [("sync", DS), ("scalar", DWS[8]), ("gpsimd", DWS[9])]
        S.wait("sync", VE, VE.n)
        S.wait("scalar", VE, VE.n)
        S.wait("gpsimd", VE, VE.n)
        for c in range(DC):
            eng, ctr = qs[c % 3]
            S.op(eng, lambda e, c=c: e.dma_start(
                yout.ap()[c * 128:(c + 1) * 128, :], Xc(c)), ctr)
        for eng, ctr in qs:
            S.wait("sync", ctr, ctr.n)
        S.emit(nc, "out")

    return nc


def prepare_in_maps(inputs, T, n_cores):
    x = np.asarray(inputs["L0_attn_x"], dtype=np.float32)
    w_qkv = np.ascontiguousarray(np.asarray(inputs["L0_attn_w_qkv"], dtype=np.float32))
    w_out = np.asarray(inputs["L0_attn_w_out"], dtype=np.float32)
    head_scales = np.asarray(inputs["L0_attn_head_scales"], dtype=np.float32)
    nw = np.asarray(inputs["L0_ffn_norm_weight"], dtype=np.float32)
    rw = np.asarray(inputs["L0_ffn_router_weight"], dtype=np.float32)
    guw = np.asarray(inputs["L0_ffn_gate_up_weight"], dtype=np.float32)
    dww = np.asarray(inputs["L0_ffn_down_weight"], dtype=np.float32)
    alpha = np.asarray(inputs["L0_ffn_residual_alpha"], dtype=np.float32).reshape(1, 1)

    gup_s = np.empty((E, NPAIR, 128, 2048), np.float32)
    for e in range(E):
        WT = guw[e].T
        G = WT[:, :HID].reshape(DC, 128, NPAIR, 128)
        U = WT[:, HID:].reshape(DC, 128, NPAIR, 128)
        pair = np.concatenate([G, U], axis=3)
        gup_s[e] = pair.transpose(2, 1, 0, 3).reshape(NPAIR, 128, 2048)
    dwn_s = np.empty((E, NMD, 128, HID), np.float32)
    for e in range(E):
        DT = dww[e].T
        A = DT.reshape(NPAIR, 128, NMD, 128)
        dwn_s[e] = A.transpose(2, 1, 0, 3).reshape(NMD, 128, HID)
    Q = w_qkv.reshape(DC, 128, 3, H, 128)
    wqk_s = np.ascontiguousarray(Q.transpose(2, 3, 1, 0, 4).reshape(3, H, 128, 1024))
    O = w_out.reshape(DC, 128, NMD, 128)
    wo_s = np.ascontiguousarray(O.transpose(2, 1, 0, 3).reshape(NMD, 128, 1024))
    rtr_s = np.ascontiguousarray(rw.T.reshape(DC, 128, E).transpose(1, 0, 2).reshape(128, DC * E))
    nw_s = np.ascontiguousarray(nw.reshape(DC, 128).T)
    hs_s = head_scales.reshape(1, H)
    t = np.array([[(i >> 2 & 1) * 2 - 1, (i >> 1 & 1) * 2 - 1, (i & 1) * 2 - 1]
                  for i in range(8)], dtype=np.float32) / np.sqrt(3.0, dtype=np.float32)
    dirs = np.zeros((128, H), np.float32)
    dirs[0:3, :] = t.T
    shared = dict(
        gup=gup_s, dwn=dwn_s, wqk=wqk_s, wo=wo_s, rtr=rtr_s,
        nwc=nw_s, hsr=hs_s, alr=alpha,
        dirs_t=dirs.astype(ml_dtypes.bfloat16),
        id_bf=np.eye(128, dtype=ml_dtypes.bfloat16),
        id_f32=np.eye(8, dtype=np.float32),
        ones_r=np.ones((1, 128), np.float32),
        ones_rb=np.ones((1, 128), ml_dtypes.bfloat16),
        ones_c=np.ones((128, 1), np.float32),
        ones_cb=np.ones((128, 1), ml_dtypes.bfloat16),
        cmask=np.where(np.arange(128)[:, None] >= np.arange(128)[None, :],
                       0.0, NEG_BIG).astype(ml_dtypes.bfloat16),
        cst=np.tile(np.array([[1.0 + EPS_RMS, EPS_LN * EPS_RMS, -0.5, EPS_LN,
                               1.0 / math.sqrt(HD), 0.5, 0.0, 0.0]], np.float32),
                    (128, 1)),
        halves_r=np.full((1, 128), 0.5, np.float32),
    )
    in_maps = []
    for b in range(n_cores):
        m = dict(shared)
        m["xT"] = np.ascontiguousarray(x[b, :T, :].T)
        in_maps.append(m)
    return in_maps


def kernel(**inputs):
    T = 2048
    n_cores = 8
    in_maps = prepare_in_maps(inputs, T, n_cores)
    nc = build_program(T)
    res = run_bass_kernel_spmd(nc, in_maps, core_ids=list(range(n_cores)))
    out = np.empty((n_cores, T, D), np.float32)
    for b in range(n_cores):
        out[b] = res.results[b]["yout"].T
    return out
